# version 27
# baseline (speedup 1.0000x reference)
import contextlib
import numpy as np
import ml_dtypes

import concourse.bass as bass
import concourse.mybir as mybir
from concourse.alu_op_type import AluOpType as ALU
from concourse.bass_utils import run_bass_kernel_spmd

F32 = mybir.dt.float32
BF16 = mybir.dt.bfloat16
I32 = mybir.dt.int32
AF = mybir.ActivationFunctionType
AX = mybir.AxisListType
ds = bass.ds

NC = 8
D = 1024
TOPK = 2
EPS = 1e-5


class Cfg:
    def __init__(self, S=8192, FF=4096):
        self.S = S
        self.FF = FF
        self.T = 2 * S
        self.MT = S // 2
        self.shard = self.T // NC
        self.pA = self.MT // NC
        self.pB = self.shard - self.pA
        self.QT = 512
        self.KC = 128
        self.DT = 512
        self.MG = min(1024, self.MT)
        self.NG = self.MT // self.MG
        self.FFC = FF // 128
        self.NDT = self.T // self.DT
        self.NQT = S // self.QT
        self.NST = self.shard // 128
        self.npk = self.QT // self.KC
        self.NMT = self.MT // 128
        self.wbs = [512] * (self.pB // 512)
        if self.pB % 512:
            self.wbs.append(self.pB % 512)
        assert S % 512 == 0 and self.pA % 128 == 0 and self.pB % 128 == 0
        assert self.pA // 128 == self.NG
        assert self.pA <= 512 and FF % 256 == 0 and self.MG % 512 == 0


class _DummyInst:
    def then_inc(self, *a, **k):
        return self


class _DummyEng:
    def partition_id(self):
        return 0

    def compute_val(self, v):
        return 0

    def __getattr__(self, name):
        def f(*a, **k):
            return _DummyInst()
        return f


_COMPUTE = {
    "tensor_tensor", "tensor_scalar", "tensor_scalar_mul", "tensor_scalar_add",
    "scalar_tensor_tensor", "tensor_copy", "reduce_sum", "reduce_max",
    "reciprocal", "memset", "activation", "select", "tensor_reduce",
    "tensor_add", "tensor_sub", "tensor_mul", "copy", "square", "sqrt",
}


class AutoEng:

    def __init__(self, e, C, sem):
        self.e = e
        self.C = C
        self.sem = sem

    def wait_ge(self, *a, **k):
        return self.e.wait_ge(*a, **k)

    def partition_id(self):
        return self.e.partition_id()

    def compute_val(self, v):
        return self.e.compute_val(v)

    def __getattr__(self, name):
        f = getattr(self.e, name)
        if name not in _COMPUTE:
            return f
        C, sem, e = self.C, self.sem, self.e

        def g(*a, **k):
            if C.emit and C.counts[sem] > 0:
                e.wait_ge(C.sems[sem], C.counts[sem])
            inst = f(*a, **k)
            C.counts[sem] += 1
            if C.emit:
                inst.then_inc(C.sems[sem], 1)
            return inst
        return g


class Ctl:
    def __init__(self):
        self.marks = {}
        self.emit = False
        self.counts = {}
        self.sems = {}

    def start_pass(self, emit, sems):
        self.emit = emit
        self.counts = {k: 0 for k in sems} if sems else {}
        self.sems = sems

    def inc(self, inst, sem, n=1):
        self.counts[sem] += n
        if self.emit:
            inst.then_inc(self.sems[sem], n)
        return inst

    def mark(self, label, sem):
        v = (sem, self.counts[sem])
        if not self.emit:
            assert label not in self.marks, f"duplicate mark {label}"
            self.marks[label] = v
        else:
            assert self.marks[label] == v, (label, self.marks[label], v)

    def wait(self, eng, label):
        if self.emit:
            sem, v = self.marks[label]
            if v > 0:
                eng.wait_ge(self.sems[sem], v)


def build_program(cfg: Cfg):
    nc = bass.Bass("TRN2", target_bir_lowering=False)
    S, T, MT, FF = cfg.S, cfg.T, cfg.MT, cfg.FF
    pA, pB, shard = cfg.pA, cfg.pB, cfg.shard
    QT, KC, DT = cfg.QT, cfg.KC, cfg.DT
    MG, NG, FFC = cfg.MG, cfg.NG, cfg.FFC
    NDT, NQT, NST, npk, NMT = cfg.NDT, cfg.NQT, cfg.NST, cfg.npk, cfg.NMT
    wbs = cfg.wbs
    NVC = DT // KC
    NA = pA // 128
    wa = min(512, pA)
    nta = pA // wa
    MGC = MG // 128
    NMH = MG // 512
    NBT = pB // 128

    xTt = nc.dram_tensor("xTt", [(T // DT) * 128, 8 * DT], F32, kind="ExternalInput").ap()
    xshard = nc.dram_tensor("xshard", [shard, D], F32, kind="ExternalInput").ap()
    xTshard = nc.dram_tensor("xTshard", [D, shard], F32, kind="ExternalInput").ap()
    qkvwT = nc.dram_tensor("qkvwT", [D, 384], F32, kind="ExternalInput").ap()
    qkvb = nc.dram_tensor("qkvb", [128, 3], F32, kind="ExternalInput").ap()
    outwT = nc.dram_tensor("outwT", [D, D], F32, kind="ExternalInput").ap()
    outb = nc.dram_tensor("outb", [128, 8], F32, kind="ExternalInput").ap()
    ffnb = nc.dram_tensor("ffnb", [128, 8], F32, kind="ExternalInput").ap()
    ffnb_row = nc.dram_tensor("ffnb_row", [1, D], F32, kind="ExternalInput").ap()
    w1 = nc.dram_tensor("w1", [D, FF], F32, kind="ExternalInput").ap()
    w2 = nc.dram_tensor("w2", [FF, D], F32, kind="ExternalInput").ap()
    midx = nc.dram_tensor("midx", [MT, TOPK], I32, kind="ExternalInput").ap()
    mew = nc.dram_tensor("mew", [MT, TOPK], F32, kind="ExternalInput").ap()
    eidcol = nc.dram_tensor("eidcol", [128, 1], F32, kind="ExternalInput").ap()
    trigT = nc.dram_tensor("trigT", [(S // DT) * 128, 2 * DT], F32, kind="ExternalInput").ap()
    masks = nc.dram_tensor("masks", [128, 4 * 2 * QT], BF16, kind="ExternalInput").ap()
    identb = nc.dram_tensor("identb", [128, 128], BF16, kind="ExternalInput").ap()
    identf = nc.dram_tensor("identf", [128, 128], F32, kind="ExternalInput").ap()
    out_shard = nc.dram_tensor("out_shard", [shard, D], F32, kind="ExternalOutput").ap()

    qkTd = [nc.dram_tensor(f"qkT{g}", [128, 2 * S], BF16) for g in range(2)]
    vpd = [nc.dram_tensor(f"vpd{g}", [128, 2 * 65 * (S // KC)], BF16) for g in range(2)]
    w1bf = nc.dram_tensor("w1bf", [FFC * 128, 8 * 128], BF16)
    w2bf = nc.dram_tensor("w2bf", [2 * 128, FFC * 512], BF16)
    owbf = nc.dram_tensor("owbf", [128, 8 * D], BF16)
    stats_in = nc.dram_tensor("stats_in", [2, shard], F32)
    stats_out = nc.dram_tensor("stats_out", [2 * NC, shard], F32)
    contrib = nc.dram_tensor("contrib", [128, T], BF16)
    ag_out = nc.dram_tensor("ag_out", [NC * 128, T], BF16)
    ln2st = nc.dram_tensor("ln2st", [2 * NA, 128], F32)
    ln2_in = nc.dram_tensor("ln2_in", [D, pA], BF16)
    ln2_out = nc.dram_tensor("ln2_out", [NC * D, pA], BF16)
    rsg_in = [nc.dram_tensor(f"rsg_in{g}", [NC * 128, D], F32) for g in range(NG)]
    rsg_out = [nc.dram_tensor(f"rsg_out{g}", [128, D], F32) for g in range(NG)]

    rg = [list(range(NC))]
    es = contextlib.ExitStack()

    def sbuf(name, shape, dt):
        return es.enter_context(nc.sbuf_tensor(name, list(shape), dt))

    wq_sb = sbuf("wq_sb", (128, 8 * 384), BF16)
    qb_sb = sbuf("qb_sb", (128, 3), F32)
    ob_sb = sbuf("ob_sb", (128, 8), F32)
    fb_sb = sbuf("fb_sb", (128, 8), F32)
    ncs_sb = sbuf("ncs_sb", (128, 3), F32)
    eid_sb = sbuf("eid_sb", (128, 1), F32)
    idb_sb = sbuf("idb_sb", (128, 128), BF16)
    idf_sb = sbuf("idf_sb", (128, 128), F32)
    onesf_sb = sbuf("onesf_sb", (1, 128), F32)
    onesbr_sb = sbuf("onesbr_sb", (1, 128), BF16)
    onesb_sb = sbuf("onesb_sb", (128, 1), BF16)
    mcol_sb = sbuf("mcol_sb", (128, NMT), F32)
    biasbc_sb = sbuf("biasbc_sb", (128, D), F32)
    scol_sb = sbuf("scol_sb", (128, 2 * max(NST, NA)), F32)
    acc_sb = sbuf("acc_sb", (128, 4 * max(NST, NA)), F32)
    trash_sb = sbuf("trash_sb", (128, 2048), F32)
    srow_sb = sbuf("srow_sb", (2 * NST, 128), F32)
    idxf_sb = sbuf("idxf_sb", (128, 2 * NMT), F32)
    iw_sb = sbuf("iw_sb", (128, 2 * NMT), I32)
    ewt_sb = sbuf("ewt_sb", (128, 2 * NMT), F32)
    mtmp_sb = sbuf("mtmp_sb", (128, NMT), F32)
    eps_sb = sbuf("eps_sb", (128, 1), F32)
    x2At_sb = sbuf("x2At_sb", (128, NA * D), F32)

    ARENA_CAP = 174336

    class Arena:
        def __init__(self):
            self.t = sbuf("arena", (128, ARENA_CAP), mybir.dt.uint8)
            self.off = 0
            self.peak = 0

        def alloc(self, cols, dt):
            nbytes = cols * mybir.dt.size(dt)
            o = (self.off + 63) & ~63
            self.off = o + nbytes
            self.peak = max(self.peak, self.off)
            assert self.off <= ARENA_CAP, (self.off, ARENA_CAP)
            return self.t.ap()[:, o:o + nbytes].bitcast(dt)

        def save(self):
            return self.off

        def restore(self, o):
            self.off = o

    A = Arena()
    t1_sb = trash_sb.ap()[:, 1024:1536].bitcast(F32)
    t3_sb = trash_sb.ap()[:, 1536:2048].bitcast(F32)
    rs_sb = [A.alloc(D, F32) for _ in range(2)]
    finA_sb = [A.alloc(D, F32) for _ in range(2)]
    m0 = A.save()
    xt_f32 = [A.alloc(8 * DT, F32) for _ in range(4)]
    xt_bf = [A.alloc(8 * DT, BF16) for _ in range(4)]
    qst_q = [A.alloc(DT, BF16) for _ in range(2)]
    qst_k = [A.alloc(DT, BF16) for _ in range(2)]
    v_bf = [A.alloc(DT, BF16) for _ in range(2)]
    vp_st = [A.alloc(2 * 65 * NVC, BF16) for _ in range(2)]
    trig_sb = [A.alloc(2 * DT, F32) for _ in range(4)]
    shq_sb = [A.alloc(DT, F32) for _ in range(2)]
    shk_sb = [A.alloc(DT, F32) for _ in range(2)]
    valq_sb = [A.alloc(DT, F32) for _ in range(2)]
    valk_sb = [A.alloc(DT, F32) for _ in range(2)]
    valv_sb = [A.alloc(DT, F32)] * 2
    abc_sb = [A.alloc(DT, F32) for _ in range(4)]
    cbc_sb = [A.alloc(DT, F32) for _ in range(4)]
    A.restore(m0)
    qkT_g = A.alloc(2 * S, BF16)
    vpA_g = A.alloc(65 * (S // KC), BF16)
    vpB_g = A.alloc(65 * (S // KC), BF16)
    qkT_sb = [qkT_g, qkT_g]
    vpA_sb = [vpA_g, vpA_g]
    vpB_sb = [vpB_g, vpB_g]
    pT_sb = [A.alloc(2 * QT, BF16) for _ in range(4)]
    mask_sb = A.alloc(4 * 2 * QT, BF16)
    ct2_sb = [A.alloc(2 * QT, BF16) for _ in range(2)]
    outU_sb = A.alloc(4 * QT, BF16)
    rcr_sb = A.alloc(2 * QT, F32)
    A.restore(m0)
    attnA_sb = A.alloc(8 * pA, BF16)
    attnB_sb = A.alloc(8 * pB, BF16)
    ow_sb = A.alloc(8 * D, BF16)
    x2A_sb = A.alloc(8 * pA, F32)
    x2Bt_sb = A.alloc(8 * 512, F32)
    xts_sb = [A.alloc(512, F32) for _ in range(2)]
    t1b_sb = A.alloc(512, F32)
    finB_sb = [A.alloc(D, F32) for _ in range(2)]
    ln2p_sb = A.alloc(8 * pA, BF16)
    abc2_sb = A.alloc(pA, F32)
    cbc2_sb = A.alloc(pA, F32)
    A.restore(m0)
    h_sb = A.alloc(FFC * MG, BF16)
    w2_bf = [A.alloc(FFC * 512, BF16) for _ in range(2)]
    ln2g_sb = A.alloc(8 * MG, BF16)
    w1_bf = [A.alloc(8 * 128, BF16) for _ in range(2)]
    ys_st = [A.alloc(512, F32) for _ in range(2)]
    A.restore(m0)

    def stats_segments(tok0, length):
        segs = []
        t = tok0
        while t < tok0 + length:
            if t < MT:
                r = t // pA
                col = t - r * pA
                lim = min(tok0 + length, (r + 1) * pA)
            else:
                off = t - MT
                r = off // pB
                col = pA + off - r * pB
                lim = min(tok0 + length, MT + (r + 1) * pB)
            segs.append((2 * r, col, t - tok0, lim - t))
            t = lim
        return segs

    sems = {}
    for name in ("sy", "gp", "cc", "te", "ve", "sc",
                 "s_c", "s_x", "x_s2", "s_p", "trig0", "trig1",
                 "xf0", "xf1", "bc0", "bc1", "pcw", "s_gv",
                 "spq0", "spq1", "spk0", "spk1", "spv0", "spv1",
                 "s_g0", "s_g1", "s_ct", "s_ct1", "s_ag", "xts0", "xts1", "fin0", "fin1",
                 "l2", "s_pub", "ys0", "ys1", "rs0", "rs1", "fA0", "fA1",
                 "g_w", "ga", "gb", "g_w2", "g_w2b", "g_l", "g_ow"):
        sems[name] = es.enter_context(nc.semaphore(name))

    psA = es.enter_context(nc.psum_tensor("psA", [128, 1024], F32))
    psB = es.enter_context(nc.psum_tensor("psB", [128, 1024], F32))
    psC = es.enter_context(nc.psum_tensor("psC", [128, 1024], F32))
    psD = es.enter_context(nc.psum_tensor("psD", [128, 1024], F32))

    C = Ctl()

    def wslc(dc, ft):
        return wq_sb[:, dc * 384 + ft * 128: dc * 384 + (ft + 1) * 128]

    att_sched = []
    for g in range(2):
        for j in range(NQT):
            nch = npk * (j + 1)
            for i in range(nch):
                att_sched.append((g, j, i, i >= nch - npk, i == 0, i == nch - 1))
    NCH = len(att_sched)

    def xt_load(eng, j):
        sem = "xf0" if j % 2 == 0 else "xf1"
        C.inc(eng.dma_start(out=xt_f32[j % 4][:, :],
                            in_=xTt[j * 128:(j + 1) * 128, :]), sem, 16)
        C.mark(f"xf_{j}", sem)

    def bc_load(eng, j):
        sem = "bc1" if j % 2 else "bc0"
        for (row, col, dst, nn) in stats_segments(j * DT, DT):
            C.inc(eng.dma_start(
                out=abc_sb[j % 4][:, dst:dst + nn].unsqueeze(1),
                in_=stats_out[row:row + 1, col:col + nn].partition_broadcast(128)),
                sem, 16)
            C.inc(eng.dma_start(
                out=cbc_sb[j % 4][:, dst:dst + nn].unsqueeze(1),
                in_=stats_out[row + 1:row + 2, col:col + nn].partition_broadcast(128)),
                sem, 16)
        C.mark(f"bcd_{j}", sem)

    def sync_stream(e):
        for ap_in, t in ((qkvb, qb_sb), (outb, ob_sb), (ffnb, fb_sb),
                         (identb, idb_sb), (identf, idf_sb),
                         (eidcol, eid_sb)):
            C.inc(e.dma_start(out=t[:, :], in_=ap_in[:, :]), "s_c", 16)
        C.mark("consts", "s_c")

        xt_load(e, 0)
        for i in range(0, NST, 2):
            if i >= 2:
                C.wait(e, f"sq_{i - 2}")
            C.inc(e.dma_start(out=trash_sb[:, (i % 2) * 1024:(i % 2) * 1024 + 1024],
                              in_=xshard[i * 128:(i + 1) * 128, :]), "s_x", 16)
            C.mark(f"xsh_{i}", "s_x")
        xt_load(e, 2)
        C.wait(e, "srow_ready")
        C.inc(e.dma_start(
            out=stats_in.ap().rearrange("r (g f) -> (r g) f", g=NST),
            in_=srow_sb[:, :]), "s_p", 16)
        C.mark("stats_pub", "s_p")
        C.wait(e, "cc_stats")

        nv65 = 65 * NVC
        nvS = 65 * (S // KC)
        for i in range(NDT):
            b = (i * DT) // S
            s0 = i * DT - b * S
            C.wait(e, f"rope_q_{i}")
            C.inc(e.dma_start(out=qkTd[b][:, s0:s0 + DT], in_=qst_q[i % 2][:, :]),
                  f"spq{i % 2}", 16)
            C.mark(f"spill_q_{i}", f"spq{i % 2}")
            C.wait(e, f"rope_k_{i}")
            C.inc(e.dma_start(out=qkTd[b][:, S + s0:S + s0 + DT], in_=qst_k[i % 2][:, :]),
                  f"spk{i % 2}", 16)
            C.mark(f"spill_k_{i}", f"spk{i % 2}")
            C.wait(e, f"vpst_{i}")
            c0 = (s0 // KC) * 65
            vsem = f"spv{i % 2}"
            C.inc(e.dma_start(out=vpd[b][:, c0:c0 + nv65],
                              in_=vp_st[i % 2][:, 0:nv65]), vsem, 16)
            C.inc(e.dma_start(out=vpd[b][:, nvS + c0: nvS + c0 + nv65],
                              in_=vp_st[i % 2][:, nv65:2 * nv65]), vsem, 16)
            C.mark(f"spill_{i}", vsem)
            if i + 4 < NDT and (i + 4) % 2 == 0:
                C.wait(e, f"xcast_{i}")
                C.wait(e, f"xcastv_{i}")
                xt_load(e, i + 4)
            if i == 0:
                bc_load(e, 1)
            if i + 3 < NDT and (i + 3) % 2 == 1:
                if i >= 2:
                    C.wait(e, f"fix_done_{i - 1}")
                bc_load(e, i + 3)

        for g in range(2):
            for li in (NDT - 1, NDT - 2):
                C.wait(e, f"spill_{li}")
                C.wait(e, f"spill_q_{li}")
                C.wait(e, f"spill_k_{li}")
            C.wait(e, f"qkvmm_{NDT - 1}")
            C.wait(e, f"vpst_{NDT - 1}")
            C.wait(e, f"xcast_{NDT - 1}")
            C.wait(e, f"xcastv_{NDT - 1}")
            if g == 0:
                C.inc(e.dma_start(out=mask_sb[:, :], in_=masks[:, :]), "s_c", 16)
                C.mark("maskld", "s_c")
            if g == 1:
                C.wait(e, f"PV_{NCH // 2 - 1}")
            gsem = f"s_g{g}"
            C.inc(e.dma_start(out=qkT_sb[g][:, :], in_=qkTd[g][:, :]), gsem, 16)
            C.mark(f"grp_{g}", gsem)
            for j in range(NQT):
                C.wait(e, f"ct_{g}_{j}")
                tok0 = g * S + j * QT
                csem = "s_ct" if j % 2 == 0 else "s_ct1"
                C.inc(e.dma_start(out=contrib[:, tok0:tok0 + QT],
                                  in_=ct2_sb[j % 2][:, 0:QT]), csem, 16)
                C.mark(f"ct_st_{g}_{j}", csem)
            C.mark(f"ct_g{g}a", "s_ct")
            C.mark(f"ct_g{g}b", "s_ct1")

        C.wait(e, f"PV_{NCH - 1}")
        C.wait(e, "pcow")
        C.inc(e.dma_start(out=ow_sb[:, :], in_=owbf[:, :]), "g_ow", 16)
        C.mark("owld", "g_ow")

        rank = e.partition_id() if C.emit else 0
        baseA = e.compute_val(rank * pA) if C.emit else 0
        baseB = e.compute_val(rank * pB + MT) if C.emit else 0
        C.wait(e, "cc_ag")
        C.inc(e.dma_start(
            out=attnA_sb.rearrange("p (k t) -> p k t", k=8),
            in_=ag_out.ap().rearrange("(k p) t -> p k t", p=128)[:, :, ds(baseA, pA)]), "s_ag", 16)
        C.inc(e.dma_start(
            out=attnB_sb.rearrange("p (k t) -> p k t", k=8),
            in_=ag_out.ap().rearrange("(k p) t -> p k t", p=128)[:, :, ds(baseB, pB)]), "s_ag", 16)
        C.inc(e.dma_start(out=biasbc_sb[:, :].unsqueeze(1),
                          in_=ffnb_row[0:1, :].partition_broadcast(128)), "s_ag", 16)
        C.mark("biasbc", "s_ag")
        C.mark("ag_sl", "s_ag")
        for dc in range(8):
            for t in range(nta):
                k = dc * nta + t
                if k >= 2:
                    C.wait(e, f"x2A_{k - 2}")
                C.inc(e.dma_start(out=xts_sb[k % 2][:, 0:wa],
                                  in_=xTshard[dc * 128:(dc + 1) * 128,
                                              t * wa:(t + 1) * wa]), f"xts{k % 2}", 16)
                C.mark(f"xtsA_{k}", f"xts{k % 2}")
        off = pA
        for ti, wb in enumerate(wbs):
            for dc in range(8):
                k = ti * 8 + dc
                if k >= 2:
                    C.wait(e, f"x2B_{k - 2}")
                else:
                    C.wait(e, f"x2A_{8 * nta - 2 + k}")
                C.inc(e.dma_start(out=xts_sb[k % 2][:, 0:wb],
                                  in_=xTshard[dc * 128:(dc + 1) * 128, off:off + wb]),
                      f"xts{k % 2}", 16)
                C.mark(f"xtsB_{k}", f"xts{k % 2}")
            for sub in range(wb // 128):
                bt = (off - pA) // 128 + sub
                C.wait(e, f"finB_{bt}")
                C.inc(e.dma_start(out=out_shard[pA + bt * 128: pA + (bt + 1) * 128, :],
                                  in_=finB_sb[bt % 2][:, :]), f"fin{bt % 2}", 16)
                C.mark(f"outB_{bt}", f"fin{bt % 2}")
            off += wb
        C.mark("outB0", "fin0")
        C.mark("outB1", "fin1")

        C.wait(e, "l2cols")
        C.inc(e.dma_start(out=ln2st.ap().rearrange("g p -> p g"),
                          in_=scol_sb[:, 0:2 * NA],
                          allow_slow_non_contiguous=True), "l2", 16)
        C.mark("l2st_w", "l2")
        C.wait(e, "l2st_w")
        l2r = ln2st.ap().rearrange("(r g) f -> r (g f)", r=2)
        C.inc(e.dma_start(out=abc2_sb[:, :].unsqueeze(1),
                          in_=l2r[0:1, :].partition_broadcast(128)), "l2", 16)
        C.inc(e.dma_start(out=cbc2_sb[:, :].unsqueeze(1),
                          in_=l2r[1:2, :].partition_broadcast(128)), "l2", 16)
        C.mark("l2bc", "l2")

        C.wait(e, "ln2p")
        C.inc(e.dma_start(
            out=ln2_in.ap().rearrange("(c p) t -> p c t", p=128),
            in_=ln2p_sb.rearrange("p (c t) -> p c t", c=8)), "s_pub", 16)
        C.mark("ln2_pub", "s_pub")

        for g in range(NG):
            for hf in range(2):
                for tc in range(MGC):
                    kk = (g * 2 + hf) * MGC + tc
                    C.wait(e, f"ys_ev_{kk}")
                    C.inc(e.dma_start(
                        out=rsg_in[g][tc * 128:(tc + 1) * 128, hf * 512:(hf + 1) * 512],
                        in_=ys_st[kk % 2][:, :]), f"ys{kk % 2}", 16)
                    C.mark(f"ys_st_{kk}", f"ys{kk % 2}")
            C.mark(f"rsin_{g}_a", "ys0")
            C.mark(f"rsin_{g}_b", "ys1")
        if C.emit:
            for s in ("s_ct", "s_ct1", "ys0", "ys1", "s_p", "s_pub",
                      "spq0", "spq1", "spk0", "spk1", "spv0", "spv1", "l2",
                      "xf0", "bc1", "g_w2", "g_w2b", "g_ow", "s_ag"):
                if C.counts[s] > 0:
                    e.wait_ge(sems[s], C.counts[s])

    def gpsimd_stream(e):
        C.inc(e.dma_start(out=wq_sb.ap().rearrange("p (c f) -> p c f", c=8),
                          in_=qkvwT.rearrange("(c p) f -> p c f", p=128)), "g_w", 16)
        for i in range(NMT):
            C.inc(e.dma_start(out=iw_sb[:, 2 * i:2 * i + 2],
                              in_=midx[i * 128:(i + 1) * 128, :]), "g_w", 16)
            C.inc(e.dma_start(out=ewt_sb[:, 2 * i:2 * i + 2],
                              in_=mew[i * 128:(i + 1) * 128, :]), "g_w", 16)
        C.mark("wconsts", "g_w")
        C.mark("midx", "g_w")

        C.wait(e, "stats_pub")
        C.inc(e.collective_compute("AllGather", mybir.AluOpType.bypass,
                                   replica_groups=rg, ins=[stats_in.ap().opt()],
                                   outs=[stats_out.ap().opt()]), "cc", 1)
        C.mark("cc_stats", "cc")

        C.wait(e, f"xcast_{NDT - 1}")
        C.inc(e.dma_start(out=owbf.ap().rearrange("p (c d) -> p c d", c=8),
                          in_=outwT.rearrange("(c p) d -> p c d", p=128)), "pcw", 16)
        C.mark("pcow", "pcw")
        for ffc in range(FFC):
            C.inc(e.dma_start(
                out=w1bf.ap()[ffc * 128:(ffc + 1) * 128, :].rearrange(
                    "p (c f) -> p c f", c=8),
                in_=w1.rearrange("(c p) f -> p c f", p=128)[:, :, ffc * 128:(ffc + 1) * 128]),
                "pcw", 16)
            C.mark(f"pcw1_{ffc}", "pcw")
        for hf in range(2):
            C.inc(e.dma_start(
                out=w2bf.ap()[hf * 128:(hf + 1) * 128, :].rearrange(
                    "p (c d) -> p c d", c=FFC),
                in_=w2.rearrange("(c p) d -> p c d", p=128)[:, :, hf * 512:(hf + 1) * 512]),
                "pcw", 16)
            C.mark(f"pcw2_{hf}", "pcw")

        C.wait(e, "ct_g1a")
        C.wait(e, "ct_g1b")
        C.inc(e.collective_compute("AllGather", mybir.AluOpType.bypass,
                                   replica_groups=rg, ins=[contrib.ap().opt()],
                                   outs=[ag_out.ap().opt()]), "cc", 1)
        C.mark("cc_ag", "cc")
        C.wait(e, "ln2_pub")
        C.inc(e.collective_compute("AllGather", mybir.AluOpType.bypass,
                                   replica_groups=rg, ins=[ln2_in.ap().opt()],
                                   outs=[ln2_out.ap().opt()]), "cc", 1)
        C.mark("cc_ln2", "cc")

        for g in range(NG):
            if g == 0:
                C.wait(e, "cc_ln2")
                C.wait(e, "outB0")
                C.wait(e, "outB1")
            else:
                C.wait(e, f"rsin_{g - 1}_a")
                C.wait(e, f"rsin_{g - 1}_b")
                C.inc(e.collective_compute("ReduceScatter", mybir.AluOpType.add,
                                           replica_groups=rg,
                                           ins=[rsg_in[g - 1].ap().opt()],
                                           outs=[rsg_out[g - 1].ap().opt()]), "cc", 1)
                C.mark(f"cc_rs_{g - 1}", "cc")
                C.wait(e, f"ysmm_g{g - 1}")
            for ffc in range(FFC):
                k = g * FFC + ffc
                if k >= 2:
                    C.wait(e, f"hmm_{k - 2}")
                if k < FFC:
                    C.wait(e, f"pcw1_{ffc}")
                gsem = "ga" if ffc % 2 == 0 else "gb"
                C.inc(e.dma_start(out=w1_bf[ffc % 2][:, :],
                                  in_=w1bf.ap()[ffc * 128:(ffc + 1) * 128, :]),
                      gsem, 16)
                C.mark(f"w1_{k}", gsem)
            for hf in range(2):
                hx = g * 2 + hf
                if hx >= 2:
                    C.wait(e, f"ysmm_h{hx - 2}")
                if g == 0:
                    C.wait(e, f"pcw2_{hf}")
                C.inc(e.dma_start(out=w2_bf[hx % 2][:, :],
                                  in_=w2bf.ap()[hf * 128:(hf + 1) * 128, :]),
                      "g_w2" if hx % 2 == 0 else "g_w2b", 16)
                C.mark(f"w2_{g}_{hf}", "g_w2" if hx % 2 == 0 else "g_w2b")

        g = NG - 1
        C.wait(e, f"rsin_{g}_a")
        C.wait(e, f"rsin_{g}_b")
        C.inc(e.collective_compute("ReduceScatter", mybir.AluOpType.add,
                                   replica_groups=rg, ins=[rsg_in[g].ap().opt()],
                                   outs=[rsg_out[g].ap().opt()]), "cc", 1)
        C.mark(f"cc_rs_{g}", "cc")
        if C.emit:
            for s in ("pcw", "ga", "gb", "g_w"):
                if C.counts[s] > 0:
                    e.wait_ge(sems[s], C.counts[s])

    def tensor_stream(e):
        C.wait(e, "wconsts")
        C.wait(e, "onesb_ready")
        mm = None
        for ft in range(3):
            for dc in range(8):
                mm = e.matmul(psD[0:128, ft:ft + 1], wslc(dc, ft), onesb_sb[0:128, 0:1],
                              start=(dc == 0), stop=(dc == 7))
        C.inc(mm, "te", 1)
        C.mark("colsum_mm", "te")

        C.wait(e, "scol_ready")
        C.wait(e, "consts")
        C.inc(e.transpose(psC[0:2 * NST, 0:128], scol_sb[:, 0:2 * NST], idf_sb[:, :]),
              "te", 1)
        C.mark("stats_tr", "te")

        for i in range(NDT):
            p = i % 2
            C.wait(e, f"xcast_{i}")
            C.wait(e, f"xcastv_{i}")
            if i == 0:
                C.wait(e, "srow_ready")
            for ft, (pst, lbl) in enumerate(((psA, "qev"), (psB, "kev"), (psC, "vev"))):
                if i >= 2:
                    C.wait(e, f"{lbl}_{i - 2}")
                tgt = pst[:, p * 512:p * 512 + 512]
                mm = None
                for dc in range(8):
                    mm = e.matmul(tgt, wslc(dc, ft), xt_bf[i % 4][:, dc * DT:(dc + 1) * DT],
                                  start=(dc == 0), stop=(dc == 7))
                C.inc(mm, "te", 1)
                C.mark(f"qkv_{i}_{ft}", "te")
            C.mark(f"qkvmm_{i}", "te")
            for ch in range(NVC):
                C.wait(e, f"vbf_{i}")
                gi = i * NVC + ch
                if gi >= 2:
                    pi, pch = (gi - 2) // NVC, (gi - 2) % NVC
                    C.wait(e, f"vtr_ev_{pi}_{pch}")
                sl = (gi % 2) * 1024 + 512
                C.inc(e.transpose(psD.ap().bitcast(BF16)[:, sl:sl + 128],
                                  v_bf[i % 2][:, ch * 128:(ch + 1) * 128], idb_sb[:, :]),
                      "te", 1)
                C.mark(f"vtr_{i}_{ch}", "te")

        pend = []
        nrm = [0]

        def emit_rcbc(gg, jj, pr, ci_last):
            qi = gg * NQT + jj
            pbank = psC if qi % 2 == 0 else psD
            x = nrm[0]
            C.wait(e, f"rc_{gg}_{jj}_{pr}")
            C.inc(e.matmul(pbank[0:64, pr * 512:pr * 512 + 512], onesf_sb[0:1, 0:64],
                           rcr_sb[0:1, (x % 2) * QT:(x % 2) * QT + QT],
                           start=True, stop=True), "te", 1)
            C.mark(f"rcbc_{gg}_{jj}_{pr}", "te")
            nrm[0] += 1

        def emit_PV(ci):
            g, j, i, masked, first, last = att_sched[ci]
            qi = g * NQT + j
            pso = psC if qi % 2 == 0 else psD
            if first:
                if qi == 1:
                    C.wait(e, f"vtr_ev_{NDT - 1}_{NVC - 1}")
                    C.wait(e, f"vpst_{NDT - 1}")
                if qi >= 2:
                    pg, pj = divmod(qi - 2, NQT)
                    C.wait(e, f"ouev_{pg}_{pj}")
                    C.wait(e, f"ctn_{2 * (qi - 2) + 1}")
            if ci == 0:
                C.wait(e, f"vtr_ev_{NDT - 1}_{NVC - 1}")
                C.wait(e, f"vpst_{NDT - 1}")
            C.wait(e, f"pm_{ci}" if masked else f"exp_{ci}")
            pb = pT_sb[ci % 4]
            e.matmul(pso[0:65, 0:QT], vpA_sb[g][:, i * 65:(i + 1) * 65], pb[:, 0:QT],
                     start=first, stop=last)
            C.inc(e.matmul(pso[0:65, QT:2 * QT], vpB_sb[g][:, i * 65:(i + 1) * 65],
                           pb[:, QT:2 * QT], start=first, stop=last), "te", 1)
            C.mark(f"PV_{ci}", "te")
            if last:
                pend.append((g, j, 0, ci))
                pend.append((g, j, 1, ci))

        pvq = []
        for ci, (g, j, i, masked, first, last) in enumerate(att_sched):
            if i == 0:
                if g == 1 and j == 0:
                    while pvq:
                        emit_PV(pvq.pop(0))
                    while pend:
                        emit_rcbc(*pend.pop(0))
                C.wait(e, f"grp_{g}")
                C.wait(e, f"grpv_{g}")
                if ci == 0:
                    C.wait(e, f"qkvmm_{NDT - 1}")
                    C.wait(e, f"fix_done_{NDT - 1}")
            if pend and i in (1, 2):
                emit_rcbc(*pend.pop(0))
            ps = psA if i % 2 == 0 else psB
            if ci >= 2:
                C.wait(e, f"exp_{ci - 2}")
            if ci == 0:
                C.wait(e, f"vtr_ev_{NDT - 1}_{NVC - 1}")
            qsl = slice(j * QT, (j + 1) * QT)
            e.matmul(ps[:, 0:QT], qkT_sb[g][0:64, S + i * KC:S + (i + 1) * KC],
                     qkT_sb[g][0:64, qsl], start=True, stop=True, tile_position=(0, 0))
            C.inc(e.matmul(ps[:, QT:2 * QT], qkT_sb[g][64:128, S + i * KC:S + (i + 1) * KC],
                           qkT_sb[g][64:128, qsl], start=True, stop=True,
                           tile_position=(64, 0)), "te", 1)
            C.mark(f"S_{ci}", "te")
            pvq.append(ci)
            if len(pvq) >= 3:
                emit_PV(pvq.pop(0))
                emit_PV(pvq.pop(0))
        while pvq:
            emit_PV(pvq.pop(0))
        while pend:
            emit_rcbc(*pend.pop(0))

        C.wait(e, "ag_sl")
        C.wait(e, "owld")
        C.wait(e, f"exp_{NCH - 1}")
        C.wait(e, f"ctn_{2 * NQT * 2 - 1}")
        for dc in range(8):
            for t in range(nta):
                k = dc * nta + t
                if k >= 2:
                    C.wait(e, f"x2A_{k - 2}")
                tgt = psA[:, (k % 2) * 512:(k % 2) * 512 + wa]
                mm = None
                for hc in range(8):
                    mm = e.matmul(tgt, ow_sb[:, hc * D + dc * 128: hc * D + (dc + 1) * 128],
                                  attnA_sb[:, hc * pA + t * wa: hc * pA + t * wa + wa],
                                  start=(hc == 0), stop=(hc == 7))
                C.inc(mm, "te", 1)
                C.mark(f"opA_{k}", "te")
        off = 0
        for ti, wb in enumerate(wbs):
            for dc in range(8):
                k = ti * 8 + dc
                if k >= 2:
                    C.wait(e, f"x2B_{k - 2}")
                else:
                    C.wait(e, f"x2A_{8 * nta - 2 + k}")
                tgt = psA[:, (k % 2) * 512:(k % 2) * 512 + wb]
                mm = None
                for hc in range(8):
                    mm = e.matmul(tgt, ow_sb[:, hc * D + dc * 128: hc * D + (dc + 1) * 128],
                                  attnB_sb[:, hc * pB + off: hc * pB + off + wb],
                                  start=(hc == 0), stop=(hc == 7))
                C.inc(mm, "te", 1)
                C.mark(f"opB_{k}", "te")
            C.wait(e, f"x2Bt_{ti}")
            for sub in range(wb // 128):
                bt = off // 128 + sub
                for dc in range(8):
                    gi = bt * 8 + dc
                    if gi >= 2:
                        C.wait(e, f"finB_ev_{(gi - 2) // 8}_{(gi - 2) % 8}")
                    C.inc(e.transpose(psB[:, (gi % 2) * 512:(gi % 2) * 512 + 128],
                                      x2Bt_sb[:, dc * 512 + sub * 128: dc * 512 + sub * 128 + 128],
                                      idf_sb[:, :]), "te", 1)
                    C.mark(f"trB_{bt}_{dc}", "te")
            off += wb

        C.wait(e, "x2A_all")
        C.wait(e, f"finB_{NBT - 1}")
        for at in range(NA):
            for dc in range(8):
                gi = at * 8 + dc
                if gi >= 2:
                    C.wait(e, f"trA_ev_{(gi - 2) // 8}_{(gi - 2) % 8}")
                C.inc(e.transpose(psB[:, (gi % 2) * 512:(gi % 2) * 512 + 128],
                                  x2A_sb[:, dc * pA + at * 128: dc * pA + at * 128 + 128],
                                  idf_sb[:, :]), "te", 1)
                C.mark(f"trA_{at}_{dc}", "te")

        for g in range(NG):
            C.wait(e, f"ln2g_{g}")
            if g == 0:
                C.wait(e, f"x2B_{len(wbs) * 8 - 1}")
            for ffc in range(FFC):
                k = g * FFC + ffc
                C.wait(e, f"w1_{k}")
                for hm in range(NMH):
                    if k * NMH + hm >= 2:
                        pk = k * NMH + hm - 2
                        C.wait(e, f"gelu_{pk // NMH}_{pk % NMH}")
                    sl = (ffc * NMH + hm) % 2
                    tgt = psA[:, sl * 512:sl * 512 + 512]
                    mm = None
                    for hc in range(8):
                        mm = e.matmul(tgt, w1_bf[ffc % 2][:, hc * 128:(hc + 1) * 128],
                                      ln2g_sb[:, hc * MG + hm * 512:hc * MG + (hm + 1) * 512],
                                      start=(hc == 0), stop=(hc == 7))
                    C.inc(mm, "te", 1)
                    C.mark(f"hps_{k}_{hm}", "te")
                C.mark(f"hmm_{k}", "te")
            for hf in range(2):
                hx = g * 2 + hf
                C.wait(e, f"w2_{g}_{hf}")
                C.wait(e, f"gelu_{g * FFC + FFC - 1}_{NMH - 1}")
                if g == 0 and hf == 0:
                    C.wait(e, f"trA_ev_{NA - 1}_7")
                for tc in range(MGC):
                    kk = hx * MGC + tc
                    if kk >= 2:
                        C.wait(e, f"ys_ev_{kk - 2}")
                    tgt = psB[:, (kk % 2) * 512:(kk % 2) * 512 + 512]
                    mm = None
                    for ffc in range(FFC):
                        mm = e.matmul(tgt, h_sb[:, ffc * MG + tc * 128:ffc * MG + (tc + 1) * 128],
                                      w2_bf[hx % 2][:, ffc * 512:(ffc + 1) * 512],
                                      start=(ffc == 0), stop=(ffc == FFC - 1))
                    C.inc(mm, "te", 1)
                    C.mark(f"ysmm_{kk}", "te")
                C.mark(f"ysmm_h{hx}", "te")
            C.mark(f"ysmm_g{g}", "te")

    def vector_stream(e):
        e.memset(onesf_sb[:, :], 1.0)
        C.mark("ones_ready", "ve")
        e.memset(eps_sb[:, :], EPS)
        C.mark("eps_ready", "ve")
        e.memset(onesb_sb[:, :], 1.0)
        e.memset(onesbr_sb[:, :], 1.0)
        C.mark("onesb_ready", "ve")
        for st in range(2):
            e.memset(vp_st[st].rearrange("p (c w) -> p c w", w=65)[:, :, 64:65], 1.0)
        C.mark("vpones", "ve")

        C.wait(e, "colsum_mm")
        e.tensor_scalar_mul(ncs_sb[:, 0:3], psD[0:128, 0:3], -1.0)
        C.mark("colsum_done", "ve")

        for i in range(NST):
            C.wait(e, f"xsh_{i}")
            e.reduce_sum(acc_sb[:, i:i + 1],
                         trash_sb[:, (i % 2) * 1024:(i % 2) * 1024 + 1024], axis=AX.X)
            C.mark(f"sum_{i}", "ve")
        C.wait(e, "sq_all")
        n = NST
        e.tensor_scalar_mul(acc_sb[:, 2 * n:3 * n], acc_sb[:, 0:n], 1.0 / D)
        e.tensor_scalar_mul(acc_sb[:, 3 * n:4 * n], acc_sb[:, n:2 * n], 1.0 / D)
        e.tensor_tensor(scol_sb[:, 0:n], acc_sb[:, 2 * n:3 * n],
                        acc_sb[:, 2 * n:3 * n], ALU.mult)
        e.tensor_tensor(acc_sb[:, 3 * n:4 * n], acc_sb[:, 3 * n:4 * n],
                        scol_sb[:, 0:n], ALU.subtract)
        C.mark("var_ready", "ve")
        C.wait(e, "rstd_done")
        e.tensor_tensor(scol_sb[:, n:2 * n], acc_sb[:, 2 * n:3 * n],
                        scol_sb[:, 0:n], ALU.mult)
        C.mark("scol_ready", "ve")
        C.wait(e, "stats_tr")
        e.tensor_copy(srow_sb[0:2 * NST, :], psC[0:2 * NST, 0:128])
        C.mark("srow_ready", "ve")

        C.wait(e, "midx")
        e.tensor_copy(idxf_sb[:, :], iw_sb[:, :])
        id3 = idxf_sb.ap().rearrange("p (n two) -> p n two", two=2)
        ew3 = ewt_sb.ap().rearrange("p (n two) -> p n two", two=2)
        e.scalar_tensor_tensor(mtmp_sb[:, :], id3[:, :, 0], eid_sb[:, 0:1],
                               ew3[:, :, 0], ALU.is_equal, ALU.mult)
        e.scalar_tensor_tensor(mcol_sb[:, :], id3[:, :, 1], eid_sb[:, 0:1],
                               ew3[:, :, 1], ALU.is_equal, ALU.mult)
        e.tensor_tensor(mcol_sb[:, :], mcol_sb[:, :], mtmp_sb[:, :], ALU.add)
        C.mark("mcol", "ve")

        def vtr_evac_dve(ii):
            pd = psD.ap().bitcast(BF16)
            if ii >= 2:
                C.wait(e, f"spill_{ii - 2}")
            for ch in range(2, NVC):
                C.wait(e, f"vtr_{ii}_{ch}")
                gi = ii * NVC + ch
                sl = (gi % 2) * 1024 + 512
                e.tensor_copy(vp_st[ii % 2][:, ch * 65:ch * 65 + 64], pd[:, sl:sl + 64])
                e.tensor_copy(vp_st[ii % 2][:, 65 * NVC + ch * 65: 65 * NVC + ch * 65 + 64],
                              pd[:, sl + 64:sl + 128])
                C.mark(f"vtr_ev_{ii}_{ch}", "ve")

        for i in range(NDT):
            p = i % 2
            if i >= 1:
                vtr_evac_dve(i - 1)
            C.wait(e, f"xf_{i}")
            if i >= 4:
                C.wait(e, f"qkvmm_{i - 4}")
            e.tensor_copy(xt_bf[i % 4][:, 6 * DT:8 * DT], xt_f32[i % 4][:, 6 * DT:8 * DT])
            C.mark(f"xcastv_{i}", "ve")
            C.wait(e, f"bcd_{i}")
            vals = (valq_sb, valk_sb, valv_sb)
            for ft, (pst, lbl) in enumerate(((psA, "qev"), (psB, "kev"), (psC, "vev"))):
                C.wait(e, f"qkv_{i}_{ft}")
                if i >= 2 and ft < 2:
                    C.wait(e, (f"shq_{i - 2}", f"shk_{i - 2}")[ft])
                if i >= 1 and ft == 2:
                    C.wait(e, f"vbf_{i - 1}")
                src_ps = pst[:, p * 512:p * 512 + 512]
                e.tensor_tensor(t1_sb[:, :], src_ps, abc_sb[i % 4][:, :], ALU.mult)
                C.mark(f"{lbl}_{i}", "ve")
                vb = vals[ft][i % 2]
                e.scalar_tensor_tensor(vb[:, :], cbc_sb[i % 4][:, :], ncs_sb[:, ft:ft + 1],
                                       t1_sb[:, :], ALU.mult, ALU.add)
                if ft < 2:
                    e.tensor_scalar_add(vb[:, :], vb[:, :], qb_sb[:, ft:ft + 1])
                C.mark(f"val{'qkv'[ft]}_{i}", "ve")
            C.mark(f"fix_done_{i}", "ve")
            for ft in range(2):
                C.wait(e, f"trig_{i}")
                C.wait(e, f"shq_{i}" if ft == 0 else f"shk_{i}")
                if i >= 2:
                    C.wait(e, f"spill_q_{i - 2}" if ft == 0 else f"spill_k_{i - 2}")
                tc = trig_sb[i % 4]
                vb = (valq_sb, valk_sb)[ft][i % 2]
                sh = (shq_sb, shk_sb)[ft][i % 2]
                e.tensor_tensor(t1_sb[:, :], vb[:, :], tc[:, 0:DT], ALU.mult)
                e.tensor_tensor(t3_sb[:, :], sh[:, :], tc[:, DT:2 * DT], ALU.mult)
                dst = qst_q[i % 2] if ft == 0 else qst_k[i % 2]
                e.tensor_tensor(dst[:, :], t1_sb[:, :], t3_sb[:, :], ALU.add)
                C.mark(f"rope_q_{i}" if ft == 0 else f"rope_k_{i}", "ve")

        vtr_evac_dve(NDT - 1)

        nrmv = [0]
        for ci, (g, j, i, masked, first, last) in enumerate(att_sched):
            if masked:
                nch = npk * (j + 1)
                kk = i - (nch - npk)
                C.wait(e, "maskld")
                C.wait(e, f"exp_{ci}")
                e.tensor_tensor(pT_sb[ci % 4][:, :], pT_sb[ci % 4][:, :],
                                mask_sb[:, kk * 2 * QT:(kk + 1) * 2 * QT], ALU.mult)
                C.mark(f"pm_{ci}", "ve")
            if last:
                qi = g * NQT + j
                pso = psC if qi % 2 == 0 else psD
                C.wait(e, f"PV_{ci}")
                sA = (j % 2) * 2
                e.tensor_copy(outU_sb[0:65, sA * QT:(sA + 1) * QT], pso[0:65, 0:QT])
                e.tensor_copy(outU_sb[0:65, (sA + 1) * QT:(sA + 2) * QT],
                              pso[0:65, QT:2 * QT])
                C.mark(f"ouev_{g}_{j}", "ve")
                for pr2 in range(2):
                    x2 = nrmv[0] + pr2
                    e.reciprocal(rcr_sb[0:1, (x2 % 2) * QT:(x2 % 2) * QT + QT],
                                 outU_sb[64:65, (sA + pr2) * QT:(sA + pr2 + 1) * QT])
                    C.mark(f"rc_{g}_{j}_{pr2}", "ve")
                if (g, j) not in ((0, 0), (0, 1)):
                    pg, pj = (g, j - 2) if j >= 2 else (g - 1, NQT - 2 + j)
                    C.wait(e, f"ct_st_{pg}_{pj}")
                for pr in range(2):
                    x = nrmv[0]
                    C.wait(e, f"rcbc_{g}_{j}_{pr}")
                    pbank = psC if qi % 2 == 0 else psD
                    e.tensor_tensor(ct2_sb[j % 2][pr * 64:(pr + 1) * 64, 0:QT],
                                    outU_sb[0:64, (sA + pr) * QT:(sA + pr + 1) * QT],
                                    pbank[0:64, pr * 512:pr * 512 + 512], ALU.mult)
                    C.mark(f"ctn_{x}", "ve")
                    nrmv[0] += 1
                C.mark(f"ct_{g}_{j}", "ve")

        for dc in range(8):
            for t in range(nta):
                k = dc * nta + t
                C.wait(e, f"opA_{k}")
                C.wait(e, f"xtsA_{k}")
                src = psA[:, (k % 2) * 512:(k % 2) * 512 + wa]
                e.tensor_scalar_add(t1b_sb[:, 0:wa], src, ob_sb[:, dc:dc + 1])
                e.tensor_tensor(x2A_sb[:, dc * pA + t * wa: dc * pA + t * wa + wa],
                                t1b_sb[:, 0:wa], xts_sb[k % 2][:, 0:wa], ALU.add)
                C.mark(f"x2A_{k}", "ve")
        C.mark("x2A_all", "ve")
        off = 0
        for ti, wb in enumerate(wbs):
            for dc in range(8):
                k = ti * 8 + dc
                C.wait(e, f"opB_{k}")
                C.wait(e, f"xtsB_{k}")
                if ti >= 1 and dc == 0:
                    C.wait(e, f"trB_{off // 128 - 1}_7")
                src = psA[:, (k % 2) * 512:(k % 2) * 512 + wb]
                e.tensor_scalar_add(t1b_sb[:, 0:wb], src, ob_sb[:, dc:dc + 1])
                e.tensor_tensor(t1b_sb[:, 0:wb], t1b_sb[:, 0:wb], xts_sb[k % 2][:, 0:wb], ALU.add)
                e.tensor_scalar_add(x2Bt_sb[:, dc * 512:dc * 512 + wb],
                                    t1b_sb[:, 0:wb], fb_sb[:, dc:dc + 1])
                C.mark(f"x2B_{k}", "ve")
            C.mark(f"x2Bt_{ti}", "ve")
            for sub in range(wb // 128):
                bt = off // 128 + sub
                if bt >= 2:
                    C.wait(e, f"outB_{bt - 2}")
                for dc in range(8):
                    C.wait(e, f"trB_{bt}_{dc}")
                    gi = bt * 8 + dc
                    e.tensor_copy(finB_sb[bt % 2][:, dc * 128:(dc + 1) * 128],
                                  psB[:, (gi % 2) * 512:(gi % 2) * 512 + 128])
                    C.mark(f"finB_ev_{bt}_{dc}", "ve")
                C.mark(f"finB_{bt}", "ve")
            off += wb

        for at in range(NA):
            for dc in range(8):
                C.wait(e, f"trA_{at}_{dc}")
                gi = at * 8 + dc
                e.tensor_copy(x2At_sb[:, at * D + dc * 128: at * D + (dc + 1) * 128],
                              psB[:, (gi % 2) * 512:(gi % 2) * 512 + 128])
                C.mark(f"trA_ev_{at}_{dc}", "ve")

        for at in range(NA):
            e.reduce_sum(acc_sb[:, at:at + 1], x2At_sb[:, at * D:(at + 1) * D], axis=AX.X)
            C.mark(f"l2sum_{at}", "ve")
        C.wait(e, "l2sq_all")
        n = NA
        e.tensor_scalar_mul(acc_sb[:, 2 * n:3 * n], acc_sb[:, 0:n], 1.0 / D)
        e.tensor_scalar_mul(acc_sb[:, 3 * n:4 * n], acc_sb[:, n:2 * n], 1.0 / D)
        e.tensor_tensor(scol_sb[:, 0:n], acc_sb[:, 2 * n:3 * n],
                        acc_sb[:, 2 * n:3 * n], ALU.mult)
        e.tensor_tensor(acc_sb[:, 3 * n:4 * n], acc_sb[:, 3 * n:4 * n],
                        scol_sb[:, 0:n], ALU.subtract)
        C.mark("l2var", "ve")
        C.wait(e, "l2rstd")
        e.tensor_copy(scol_sb[:, 0:n], acc_sb[:, 0:n])
        e.tensor_tensor(scol_sb[:, n:2 * n], acc_sb[:, 2 * n:3 * n],
                        acc_sb[:, 0:n], ALU.mult)
        C.mark("l2cols", "ve")

        C.wait(e, "l2bc")
        for dc in range(8):
            e.tensor_tensor(t1b_sb[:, 0:pA] if pA <= 512 else None,
                            x2A_sb[:, dc * pA:(dc + 1) * pA], abc2_sb[:, 0:pA], ALU.mult)
            e.tensor_tensor(ln2p_sb[:, dc * pA:(dc + 1) * pA], t1b_sb[:, 0:pA],
                            cbc2_sb[:, 0:pA], ALU.subtract)
        C.mark("ln2p", "ve")

        C.wait(e, "mcol")
        for g in range(NG):
            for hf in range(2):
                for tc in range(MGC):
                    kk = (g * 2 + hf) * MGC + tc
                    C.wait(e, f"ysmm_{kk}")
                    if kk >= 2:
                        C.wait(e, f"ys_st_{kk - 2}")
                    src = psB[:, (kk % 2) * 512:(kk % 2) * 512 + 512]
                    mi = NA * tc + g
                    e.tensor_scalar_mul(ys_st[kk % 2][:, :], src,
                                        mcol_sb[:, mi:mi + 1])
                    C.mark(f"ys_ev_{kk}", "ve")

        C.wait(e, "outB0")
        C.wait(e, "outB1")
        for g in range(NG):
            C.wait(e, f"rs_ld_{g}")
            C.wait(e, "biasbc")
            if g >= 2:
                C.wait(e, f"outA_{g - 2}")
            e.tensor_tensor(finA_sb[g % 2][:, :], rs_sb[g % 2][:, :],
                            x2At_sb[:, g * D:(g + 1) * D], ALU.add)
            e.tensor_tensor(finA_sb[g % 2][:, :], finA_sb[g % 2][:, :],
                            biasbc_sb[:, :], ALU.add)
            C.mark(f"finA_{g}", "ve")

    def scalar_stream(e):
        C.inc(e.dma_start(out=trash_sb[:, 1024:2048],
                          in_=xshard[128:256, :]), "x_s2", 16)
        C.mark("xsh_1", "x_s2")
        xt_load(e, 1)
        xt_load(e, 3)
        for i in range(NST):
            C.wait(e, f"sum_{i}")
            sl = trash_sb[:, (i % 2) * 1024:(i % 2) * 1024 + 1024]
            e.activation(sl, sl, AF.Square,
                         accum_out=acc_sb[:, NST + i:NST + i + 1])
            C.mark(f"sq_{i}", "sc")
            if i % 2 == 1 and i + 2 < NST:
                C.wait(e, f"sq_{i}")
                C.inc(e.dma_start(
                    out=trash_sb[:, 1024:2048],
                    in_=xshard[(i + 2) * 128:(i + 3) * 128, :]), "x_s2", 16)
                C.mark(f"xsh_{i + 2}", "x_s2")
        C.mark("sq_all", "sc")
        C.wait(e, "var_ready")
        C.wait(e, "eps_ready")
        n = NST
        e.activation(trash_sb[:, 0:n], acc_sb[:, 3 * n:4 * n], AF.Ln, bias=eps_sb[:, 0:1])
        e.activation(scol_sb[:, 0:n], trash_sb[:, 0:n], AF.Exp, scale=-0.5)
        C.mark("rstd_done", "sc")

        for i in range(NDT):
            b = (i * DT) // S
            s0 = i * DT - b * S
            C.wait(e, f"xf_{i}")
            if i >= 4:
                C.wait(e, f"qkvmm_{i - 4}")
            e.copy(xt_bf[i % 4][:, 0:6 * DT], xt_f32[i % 4][:, 0:6 * DT])
            C.mark(f"xcast_{i}", "sc")
            if i == 0:
                for jt in (0, 1):
                    jb = (jt * DT) % S // DT
                    C.inc(e.dma_start(out=trig_sb[jt][:, :],
                                      in_=trigT[jb * 128:(jb + 1) * 128, :]),
                          f"trig{jt % 2}", 16)
                    C.mark(f"trig_{jt}", f"trig{jt % 2}")
            if i + 2 < NDT:
                if i >= 2:
                    C.wait(e, f"rope_q_{i - 2}")
                    C.wait(e, f"rope_k_{i - 2}")
                jt = i + 2
                s2 = jt * DT - ((jt * DT) // S) * S
                jb = s2 // DT
                C.inc(e.dma_start(out=trig_sb[jt % 4][:, :],
                                  in_=trigT[jb * 128:(jb + 1) * 128, :]),
                      f"trig{jt % 2}", 16)
                C.mark(f"trig_{jt}", f"trig{jt % 2}")
            if i % 2 == 0:
                if i == 0:
                    C.wait(e, "cc_stats")
                    bc_load(e, 0)
                    bc_load(e, 2)
                elif i + 2 < NDT:
                    C.wait(e, f"fix_done_{i - 2}")
                    bc_load(e, i + 2)
            if i % 2 == 1 and i + 4 < NDT:
                C.wait(e, f"xcast_{i}")
                C.wait(e, f"xcastv_{i}")
                xt_load(e, i + 4)
            for ft in range(2):
                C.wait(e, f"val{'qk'[ft]}_{i}")
                if i >= 2:
                    C.wait(e, f"rope_q_{i - 2}" if ft == 0 else f"rope_k_{i - 2}")
                sh = (shq_sb, shk_sb)[ft][i % 2]
                vb = (valq_sb, valk_sb)[ft][i % 2]
                e.copy(sh[0:32, :], vb[32:64, :])
                e.copy(sh[32:64, :], vb[0:32, :])
                e.copy(sh[64:96, :], vb[96:128, :])
                e.copy(sh[96:128, :], vb[64:96, :])
                C.mark(f"shq_{i}" if ft == 0 else f"shk_{i}", "sc")
            C.wait(e, f"valv_{i}")
            if i >= 2:
                C.wait(e, f"vtr_{i - 2}_{NVC - 1}")
            e.activation(v_bf[i % 2][:, :], valv_sb[i % 2][:, :], AF.Identity,
                         bias=qb_sb[:, 2:3])
            C.mark(f"vbf_{i}", "sc")
            if i >= 2:
                C.wait(e, f"spill_{i - 2}")
            pd = psD.ap().bitcast(BF16)
            for ch in range(2):
                C.wait(e, f"vtr_{i}_{ch}")
                gi = i * NVC + ch
                sl = (gi % 2) * 1024 + 512
                e.copy(vp_st[i % 2][:, ch * 65:ch * 65 + 64], pd[:, sl:sl + 64])
                e.copy(vp_st[i % 2][:, 65 * NVC + ch * 65: 65 * NVC + ch * 65 + 64],
                       pd[:, sl + 64:sl + 128])
                C.mark(f"vtr_ev_{i}_{ch}", "sc")
            for ch in range(2, NVC):
                C.wait(e, f"vtr_ev_{i}_{ch}")
            C.mark(f"vpst_{i}", "sc")

        def vp_reload(g):
            nvS = 65 * (S // KC)
            for li in (NDT - 1, NDT - 2):
                C.wait(e, f"spill_{li}")
                C.wait(e, f"spill_q_{li}")
                C.wait(e, f"spill_k_{li}")
            C.wait(e, f"qkvmm_{NDT - 1}")
            C.wait(e, f"vpst_{NDT - 1}")
            C.wait(e, f"xcast_{NDT - 1}")
            C.wait(e, f"xcastv_{NDT - 1}")
            if g == 1:
                C.wait(e, f"PV_{NCH // 2 - 1}")
            C.inc(e.dma_start(out=vpA_sb[g][:, :], in_=vpd[g][:, 0:nvS]), "s_gv", 16)
            C.inc(e.dma_start(out=vpB_sb[g][:, :], in_=vpd[g][:, nvS:2 * nvS]), "s_gv", 16)
            C.mark(f"grpv_{g}", "s_gv")

        vp_reload(0)

        for lbl in (f"spill_q_{NDT - 1}", f"spill_q_{NDT - 2}",
                    f"spill_k_{NDT - 1}", f"spill_k_{NDT - 2}",
                    f"spill_{NDT - 1}", f"spill_{NDT - 2}", f"xcastv_{NDT - 1}"):
            C.wait(e, lbl)
        for ci, (g, j, i, masked, first, last) in enumerate(att_sched):
            if ci == NCH // 2:
                vp_reload(1)
            C.wait(e, f"S_{ci}")
            if ci >= 4:
                C.wait(e, f"PV_{ci - 4}")
            ps = psA if i % 2 == 0 else psB
            e.activation(pT_sb[ci % 4][:, :], ps[:, :], AF.Exp, scale=0.125)
            C.mark(f"exp_{ci}", "sc")

        for at in range(NA):
            C.wait(e, f"l2sum_{at}")
            e.activation(trash_sb[:, 0:D], x2At_sb[:, at * D:(at + 1) * D],
                         AF.Square, accum_out=acc_sb[:, NA + at:NA + at + 1])
            C.mark(f"l2sq_{at}", "sc")
        C.mark("l2sq_all", "sc")
        C.wait(e, "l2var")
        n = NA
        e.activation(trash_sb[:, 0:n], acc_sb[:, 3 * n:4 * n], AF.Ln, bias=eps_sb[:, 0:1])
        e.activation(acc_sb[:, 0:n], trash_sb[:, 0:n], AF.Exp, scale=-0.5)
        C.mark("l2rstd", "sc")

        for g in range(NG):
            if g == 0:
                C.wait(e, "cc_ln2")
                C.wait(e, "outB0")
                C.wait(e, "outB1")
            else:
                C.wait(e, f"ysmm_g{g - 1}")
            for r in range(NC):
                C.inc(e.dma_start(
                    out=ln2g_sb.rearrange("p (c t) -> p c t", c=8)[:, :, r * 128:(r + 1) * 128],
                    in_=ln2_out.ap().rearrange("(r c p) t -> p (r c) t", p=128, c=8)[
                        :, r * 8:(r + 1) * 8, 128 * g:128 * (g + 1)]), "g_l", 16)
            C.mark(f"ln2g_{g}", "g_l")
            for ffc in range(FFC):
                k = g * FFC + ffc
                for hm in range(NMH):
                    C.wait(e, f"hps_{k}_{hm}")
                    sl = (ffc * NMH + hm) % 2
                    e.activation(h_sb[:, ffc * MG + hm * 512:ffc * MG + (hm + 1) * 512],
                                 psA[:, sl * 512:sl * 512 + 512], AF.Gelu)
                    C.mark(f"gelu_{k}_{hm}", "sc")
            if g >= 1:
                gg = g - 1
                C.wait(e, f"cc_rs_{gg}")
                if gg >= 2:
                    C.wait(e, f"finA_{gg - 2}")
                C.inc(e.dma_start(out=rs_sb[gg % 2][:, :], in_=rsg_out[gg][:, :]),
                      f"rs{gg % 2}", 16)
                C.mark(f"rs_ld_{gg}", f"rs{gg % 2}")
        gg = NG - 1
        C.wait(e, f"cc_rs_{gg}")
        if gg >= 2:
            C.wait(e, f"finA_{gg - 2}")
        C.inc(e.dma_start(out=rs_sb[gg % 2][:, :], in_=rsg_out[gg][:, :]),
              f"rs{gg % 2}", 16)
        C.mark(f"rs_ld_{gg}", f"rs{gg % 2}")
        for g in range(NG):
            C.wait(e, f"finA_{g}")
            C.inc(e.dma_start(out=out_shard[g * 128:(g + 1) * 128, :],
                              in_=finA_sb[g % 2][:, :]), f"fA{g % 2}", 16)
            C.mark(f"outA_{g}", f"fA{g % 2}")
        if C.emit:
            for s in ("fA0", "fA1", "rs0", "rs1", "xf1", "bc0",
                      "trig0", "trig1", "x_s2", "s_gv", "g_l"):
                if C.counts[s] > 0:
                    e.wait_ge(C.sems[s], C.counts[s])

    streams = {
        "sy": sync_stream, "gp": gpsimd_stream, "te": tensor_stream,
        "ve": vector_stream, "sc": scalar_stream,
    }

    for emit in (False, True):
        C.start_pass(emit, sems)
        if not emit:
            sync_stream(_DummyEng())
            gpsimd_stream(_DummyEng())
            tensor_stream(_DummyEng())
            vector_stream(AutoEng(_DummyEng(), C, "ve"))
            scalar_stream(AutoEng(_DummyEng(), C, "sc"))
        else:
            block = es.enter_context(nc.Block())
            block.sync(lambda e: sync_stream(e))
            block.gpsimd(lambda e: gpsimd_stream(e))
            block.tensor(lambda e: tensor_stream(e))
            block.vector(lambda e: vector_stream(AutoEng(e, C, "ve")))
            block.scalar(lambda e: scalar_stream(AutoEng(e, C, "sc")))
    es.close()
    return nc


def make_inputs(cfg: Cfg, x, qkv_w, qkv_b, out_w, out_b, ew, idx, w1, w2, fbias):
    S, T, MT = cfg.S, cfg.T, cfg.MT
    pA, pB = cfg.pA, cfg.pB
    f32 = np.float32
    bf16 = ml_dtypes.bfloat16
    xf = np.ascontiguousarray(np.asarray(x, dtype=f32).reshape(T, D))
    NDT = T // 512
    xTt = np.ascontiguousarray(
        xf.T.reshape(8, 128, NDT, 512).transpose(2, 1, 0, 3).reshape(NDT * 128, 4096))
    outwT = np.ascontiguousarray(np.asarray(out_w, dtype=f32).T)
    outb_c = np.ascontiguousarray(np.asarray(out_b, dtype=f32).reshape(8, 128).T)
    ffnb_c = np.ascontiguousarray(np.asarray(fbias, dtype=f32).reshape(8, 128).T)
    ffnb_r = np.ascontiguousarray(np.asarray(fbias, dtype=f32).reshape(1, D))
    inv = 1.0 / (10000.0 ** (np.arange(32, dtype=f32) / np.float32(32.0)))
    ang = np.arange(S, dtype=f32)[:, None] * inv[None, :]
    cosS = np.cos(ang).T.astype(f32)
    sinS = np.sin(ang).T.astype(f32)
    cosTm = np.tile(cosS, (4, 1))
    sinTm = np.concatenate([-sinS, sinS, -sinS, sinS], axis=0)
    NJT = S // 512
    trigT = np.ascontiguousarray(np.concatenate(
        [cosTm.reshape(128, NJT, 512).transpose(1, 0, 2),
         sinTm.reshape(128, NJT, 512).transpose(1, 0, 2)], axis=2).reshape(NJT * 128, 1024))
    mk = []
    for kk in range(4):
        m = (np.arange(128)[:, None] + 128 * kk <= np.arange(cfg.QT)[None, :])
        mk.append(np.concatenate([m, m], axis=1))
    masks = np.ascontiguousarray(np.concatenate(mk, axis=1).astype(bf16))
    identb = np.eye(128, dtype=bf16)
    identf = np.eye(128, dtype=f32)
    qkv_w = np.asarray(qkv_w, dtype=f32)
    qkv_b = np.asarray(qkv_b, dtype=f32)
    w1 = np.asarray(w1, dtype=f32)
    w2 = np.asarray(w2, dtype=f32)

    in_maps = []
    for c in range(NC):
        Rc = np.r_[pA * c:pA * (c + 1), MT + pB * c: MT + pB * (c + 1)]
        sel = np.r_[128 * c:128 * (c + 1)]
        rows = np.r_[sel, 1024 + sel, 2048 + sel]
        in_maps.append({
            "xTt": xTt,
            "xshard": np.ascontiguousarray(xf[Rc]),
            "xTshard": np.ascontiguousarray(xf[Rc].T),
            "qkvwT": np.ascontiguousarray(qkv_w[rows].T),
            "qkvb": np.ascontiguousarray(qkv_b[rows].reshape(3, 128).T),
            "outwT": outwT,
            "outb": outb_c,
            "ffnb": ffnb_c,
            "ffnb_row": ffnb_r,
            "w1": np.ascontiguousarray(w1[c]),
            "w2": np.ascontiguousarray(w2[c]),
            "midx": np.ascontiguousarray(np.asarray(idx, dtype=np.int32)),
            "mew": np.ascontiguousarray(np.asarray(ew, dtype=f32)),
            "eidcol": np.full((128, 1), float(c), dtype=f32),
            "trigT": trigT,
            "masks": masks,
            "identb": identb,
            "identf": identf,
        })
    return in_maps


def assemble(cfg: Cfg, results):
    out = np.empty((cfg.T, D), dtype=np.float32)
    for c in range(NC):
        Rc = np.r_[cfg.pA * c:cfg.pA * (c + 1),
                   cfg.MT + cfg.pB * c: cfg.MT + cfg.pB * (c + 1)]
        out[Rc] = results[c]["out_shard"]
    return out.reshape(2, cfg.S, D)


def kernel(L0_attn_x, L0_attn_qkv_w, L0_attn_qkv_b, L0_attn_out_w, L0_attn_out_b,
           L0_ffn_scores, L0_ffn_expert_weights, L0_ffn_expert_indices,
           L0_ffn_w1, L0_ffn_w2, L0_ffn_bias):
    cfg = Cfg(S=np.asarray(L0_attn_x).shape[1], FF=np.asarray(L0_ffn_w1).shape[2])
    nc = build_program(cfg)
    in_maps = make_inputs(cfg, L0_attn_x, L0_attn_qkv_w, L0_attn_qkv_b,
                          L0_attn_out_w, L0_attn_out_b, L0_ffn_expert_weights,
                          L0_ffn_expert_indices, L0_ffn_w1, L0_ffn_w2, L0_ffn_bias)
    res = run_bass_kernel_spmd(nc, in_maps, core_ids=list(range(NC)))
    return assemble(cfg, res.results)


# version 28
# speedup vs baseline: 1.1337x; 1.1337x over previous
import contextlib
import numpy as np
import ml_dtypes

import concourse.bass as bass
import concourse.mybir as mybir
from concourse.alu_op_type import AluOpType as ALU
from concourse.bass_utils import run_bass_kernel_spmd

F32 = mybir.dt.float32
BF16 = mybir.dt.bfloat16
I32 = mybir.dt.int32
AF = mybir.ActivationFunctionType
AX = mybir.AxisListType
ds = bass.ds

NC = 8
D = 1024
TOPK = 2
EPS = 1e-5


class Cfg:
    def __init__(self, S=8192, FF=4096):
        self.S = S
        self.FF = FF
        self.T = 2 * S
        self.MT = S // 2
        self.shard = self.T // NC
        self.pA = self.MT // NC
        self.pB = self.shard - self.pA
        self.QT = 512
        self.KC = 128
        self.DT = 512
        self.MG = min(1024, self.MT)
        self.NG = self.MT // self.MG
        self.FFC = FF // 128
        self.NDT = self.T // self.DT
        self.NQT = S // self.QT
        self.NST = self.shard // 128
        self.npk = self.QT // self.KC
        self.NMT = self.MT // 128
        self.wbs = [512] * (self.pB // 512)
        if self.pB % 512:
            self.wbs.append(self.pB % 512)
        assert S % 512 == 0 and self.pA % 128 == 0 and self.pB % 128 == 0
        assert self.pA // 128 == self.NG
        assert self.pA <= 512 and FF % 256 == 0 and self.MG % 512 == 0


class _DummyInst:
    def then_inc(self, *a, **k):
        return self


class _DummyEng:
    def partition_id(self):
        return 0

    def compute_val(self, v):
        return 0

    def __getattr__(self, name):
        def f(*a, **k):
            return _DummyInst()
        return f


_COMPUTE = {
    "tensor_tensor", "tensor_scalar", "tensor_scalar_mul", "tensor_scalar_add",
    "scalar_tensor_tensor", "tensor_copy", "reduce_sum", "reduce_max",
    "reciprocal", "memset", "activation", "select", "tensor_reduce",
    "tensor_add", "tensor_sub", "tensor_mul", "copy", "square", "sqrt",
}


class AutoEng:

    def __init__(self, e, C, sem):
        self.e = e
        self.C = C
        self.sem = sem

    def wait_ge(self, *a, **k):
        return self.e.wait_ge(*a, **k)

    def partition_id(self):
        return self.e.partition_id()

    def compute_val(self, v):
        return self.e.compute_val(v)

    def __getattr__(self, name):
        f = getattr(self.e, name)
        if name not in _COMPUTE:
            return f
        C, sem, e = self.C, self.sem, self.e

        def g(*a, **k):
            if C.emit and C.counts[sem] > 0:
                e.wait_ge(C.sems[sem], C.counts[sem])
            inst = f(*a, **k)
            C.counts[sem] += 1
            if C.emit:
                inst.then_inc(C.sems[sem], 1)
            return inst
        return g


class Ctl:
    def __init__(self):
        self.marks = {}
        self.emit = False
        self.counts = {}
        self.sems = {}

    def start_pass(self, emit, sems):
        self.emit = emit
        self.counts = {k: 0 for k in sems} if sems else {}
        self.sems = sems

    def inc(self, inst, sem, n=1):
        self.counts[sem] += n
        if self.emit:
            inst.then_inc(self.sems[sem], n)
        return inst

    def mark(self, label, sem):
        v = (sem, self.counts[sem])
        if not self.emit:
            assert label not in self.marks, f"duplicate mark {label}"
            self.marks[label] = v
        else:
            assert self.marks[label] == v, (label, self.marks[label], v)

    def wait(self, eng, label):
        if self.emit:
            sem, v = self.marks[label]
            if v > 0:
                eng.wait_ge(self.sems[sem], v)


def build_program(cfg: Cfg):
    nc = bass.Bass("TRN2", target_bir_lowering=False)
    S, T, MT, FF = cfg.S, cfg.T, cfg.MT, cfg.FF
    pA, pB, shard = cfg.pA, cfg.pB, cfg.shard
    QT, KC, DT = cfg.QT, cfg.KC, cfg.DT
    MG, NG, FFC = cfg.MG, cfg.NG, cfg.FFC
    NDT, NQT, NST, npk, NMT = cfg.NDT, cfg.NQT, cfg.NST, cfg.npk, cfg.NMT
    wbs = cfg.wbs
    NVC = DT // KC
    NA = pA // 128
    wa = min(512, pA)
    nta = pA // wa
    MGC = MG // 128
    NMH = MG // 512
    NBT = pB // 128

    xTt = nc.dram_tensor("xTt", [(T // DT) * 128, 8 * DT], F32, kind="ExternalInput").ap()
    xshard = nc.dram_tensor("xshard", [shard, D], F32, kind="ExternalInput").ap()
    xTshard = nc.dram_tensor("xTshard", [D, shard], F32, kind="ExternalInput").ap()
    qkvwT = nc.dram_tensor("qkvwT", [D, 384], F32, kind="ExternalInput").ap()
    qkvb = nc.dram_tensor("qkvb", [128, 3], F32, kind="ExternalInput").ap()
    outwT = nc.dram_tensor("outwT", [D, D], F32, kind="ExternalInput").ap()
    outb = nc.dram_tensor("outb", [128, 8], F32, kind="ExternalInput").ap()
    ffnb = nc.dram_tensor("ffnb", [128, 8], F32, kind="ExternalInput").ap()
    ffnb_row = nc.dram_tensor("ffnb_row", [1, D], F32, kind="ExternalInput").ap()
    w1 = nc.dram_tensor("w1", [D, FF], F32, kind="ExternalInput").ap()
    w2 = nc.dram_tensor("w2", [FF, D], F32, kind="ExternalInput").ap()
    midx = nc.dram_tensor("midx", [MT, TOPK], I32, kind="ExternalInput").ap()
    mew = nc.dram_tensor("mew", [MT, TOPK], F32, kind="ExternalInput").ap()
    eidcol = nc.dram_tensor("eidcol", [128, 1], F32, kind="ExternalInput").ap()
    trigT = nc.dram_tensor("trigT", [(S // DT) * 128, 2 * DT], F32, kind="ExternalInput").ap()
    masks = nc.dram_tensor("masks", [128, 4 * 2 * QT], BF16, kind="ExternalInput").ap()
    identb = nc.dram_tensor("identb", [128, 128], BF16, kind="ExternalInput").ap()
    identf = nc.dram_tensor("identf", [128, 128], F32, kind="ExternalInput").ap()
    out_shard = nc.dram_tensor("out_shard", [shard, D], F32, kind="ExternalOutput").ap()

    qkTd = [nc.dram_tensor(f"qkT{g}", [128, 2 * S], BF16) for g in range(2)]
    vpd = [nc.dram_tensor(f"vpd{g}", [128, 2 * 65 * (S // KC)], BF16) for g in range(2)]
    w1bf = nc.dram_tensor("w1bf", [FFC * 128, 8 * 128], BF16)
    w2bf = nc.dram_tensor("w2bf", [2 * 128, FFC * 512], BF16)
    owbf = nc.dram_tensor("owbf", [128, 8 * D], BF16)
    stats_in = nc.dram_tensor("stats_in", [2, shard], F32)
    stats_out = nc.dram_tensor("stats_out", [2 * NC, shard], F32)
    contrib = nc.dram_tensor("contrib", [128, T], BF16)
    ag_out = nc.dram_tensor("ag_out", [NC * 128, T], BF16)
    ln2st = nc.dram_tensor("ln2st", [2 * NA, 128], F32)
    ln2_in = nc.dram_tensor("ln2_in", [D, pA], BF16)
    ln2_out = nc.dram_tensor("ln2_out", [NC * D, pA], BF16)
    rsg_in = [nc.dram_tensor(f"rsg_in{g}", [NC * 128, D], F32) for g in range(NG)]
    rsg_out = [nc.dram_tensor(f"rsg_out{g}", [128, D], F32) for g in range(NG)]

    rg = [list(range(NC))]
    es = contextlib.ExitStack()

    def sbuf(name, shape, dt):
        return es.enter_context(nc.sbuf_tensor(name, list(shape), dt))

    wq_sb = sbuf("wq_sb", (128, 8 * 384), BF16)
    qb_sb = sbuf("qb_sb", (128, 3), F32)
    ob_sb = sbuf("ob_sb", (128, 8), F32)
    fb_sb = sbuf("fb_sb", (128, 8), F32)
    ncs_sb = sbuf("ncs_sb", (128, 3), F32)
    eid_sb = sbuf("eid_sb", (128, 1), F32)
    idb_sb = sbuf("idb_sb", (128, 128), BF16)
    idf_sb = sbuf("idf_sb", (128, 128), F32)
    onesf_sb = sbuf("onesf_sb", (1, 128), F32)
    onesbr_sb = sbuf("onesbr_sb", (1, 128), BF16)
    onesb_sb = sbuf("onesb_sb", (128, 1), BF16)
    mcol_sb = sbuf("mcol_sb", (128, NMT), F32)
    biasbc_sb = sbuf("biasbc_sb", (128, D), F32)
    scol_sb = sbuf("scol_sb", (128, 2 * max(NST, NA)), F32)
    acc_sb = sbuf("acc_sb", (128, 4 * max(NST, NA)), F32)
    trash_sb = sbuf("trash_sb", (128, 2048), F32)
    srow_sb = sbuf("srow_sb", (2 * NST, 128), F32)
    idxf_sb = sbuf("idxf_sb", (128, 2 * NMT), F32)
    iw_sb = sbuf("iw_sb", (128, 2 * NMT), I32)
    ewt_sb = sbuf("ewt_sb", (128, 2 * NMT), F32)
    mtmp_sb = sbuf("mtmp_sb", (128, NMT), F32)
    eps_sb = sbuf("eps_sb", (128, 1), F32)
    x2At_sb = sbuf("x2At_sb", (128, NA * D), F32)

    ARENA_CAP = 170 * 1024

    class Arena:
        def __init__(self):
            self.t = sbuf("arena", (128, ARENA_CAP), mybir.dt.uint8)
            self.off = 0
            self.peak = 0

        def alloc(self, cols, dt):
            nbytes = cols * mybir.dt.size(dt)
            o = (self.off + 63) & ~63
            self.off = o + nbytes
            self.peak = max(self.peak, self.off)
            assert self.off <= ARENA_CAP, (self.off, ARENA_CAP)
            return self.t.ap()[:, o:o + nbytes].bitcast(dt)

        def save(self):
            return self.off

        def restore(self, o):
            self.off = o

    A = Arena()
    rs_sb = [A.alloc(D, F32) for _ in range(2)]
    finA_sb = [A.alloc(D, F32) for _ in range(2)]
    m0 = A.save()
    xt_f32 = [A.alloc(8 * DT, F32) for _ in range(4)]
    xt_bf = [A.alloc(8 * DT, BF16) for _ in range(2)]
    qst_q = [A.alloc(DT, BF16) for _ in range(2)]
    qst_k = [A.alloc(DT, BF16) for _ in range(2)]
    v_bf = [A.alloc(DT, BF16) for _ in range(2)]
    vp_st = [A.alloc(2 * 65 * NVC, BF16) for _ in range(2)]
    trig_sb = [A.alloc(2 * DT, F32) for _ in range(4)]
    t1_sb = A.alloc(DT, F32)
    t3_sb = A.alloc(DT, F32)
    shq_sb = [A.alloc(DT, F32) for _ in range(2)]
    shk_sb = [A.alloc(DT, F32) for _ in range(2)]
    valq_sb = [A.alloc(DT, F32) for _ in range(2)]
    valk_sb = [A.alloc(DT, F32) for _ in range(2)]
    valv_sb = [A.alloc(DT, F32) for _ in range(2)]
    abc_sb = [A.alloc(DT, F32) for _ in range(4)]
    cbc_sb = [A.alloc(DT, F32) for _ in range(4)]
    A.restore(m0)
    qkT_g = A.alloc(2 * S, BF16)
    vpA_g = A.alloc(65 * (S // KC), BF16)
    vpB_g = A.alloc(65 * (S // KC), BF16)
    qkT_sb = [qkT_g, qkT_g]
    vpA_sb = [vpA_g, vpA_g]
    vpB_sb = [vpB_g, vpB_g]
    pT_sb = [A.alloc(2 * QT, BF16) for _ in range(4)]
    mask_sb = A.alloc(4 * 2 * QT, BF16)
    ct2_sb = [A.alloc(2 * QT, BF16) for _ in range(2)]
    outU_sb = A.alloc(4 * QT, BF16)
    rcr_sb = A.alloc(2 * QT, F32)
    A.restore(m0)
    attnA_sb = A.alloc(8 * pA, BF16)
    attnB_sb = A.alloc(8 * pB, BF16)
    ow_sb = A.alloc(8 * D, BF16)
    x2A_sb = A.alloc(8 * pA, F32)
    x2Bt_sb = A.alloc(8 * 512, F32)
    xts_sb = [A.alloc(512, F32) for _ in range(2)]
    t1b_sb = A.alloc(512, F32)
    finB_sb = [A.alloc(D, F32) for _ in range(2)]
    ln2p_sb = A.alloc(8 * pA, BF16)
    abc2_sb = A.alloc(pA, F32)
    cbc2_sb = A.alloc(pA, F32)
    A.restore(m0)
    h_sb = A.alloc(FFC * MG, BF16)
    w2_bf = [A.alloc(FFC * 512, BF16) for _ in range(2)]
    ln2g_sb = A.alloc(8 * MG, BF16)
    w1_bf = [A.alloc(8 * 128, BF16) for _ in range(2)]
    ys_st = [A.alloc(512, F32) for _ in range(2)]
    A.restore(m0)

    def stats_segments(tok0, length):
        segs = []
        t = tok0
        while t < tok0 + length:
            if t < MT:
                r = t // pA
                col = t - r * pA
                lim = min(tok0 + length, (r + 1) * pA)
            else:
                off = t - MT
                r = off // pB
                col = pA + off - r * pB
                lim = min(tok0 + length, MT + (r + 1) * pB)
            segs.append((2 * r, col, t - tok0, lim - t))
            t = lim
        return segs

    sems = {}
    for name in ("sy", "gp", "cc", "te", "ve", "sc",
                 "s_c", "s_x", "x_s2", "s_p", "trig0", "trig1",
                 "xf0", "xf1", "bc0", "bc1", "pcw", "s_gv",
                 "spq0", "spq1", "spk0", "spk1", "spv0", "spv1",
                 "s_g0", "s_g1", "s_ct", "s_ct1", "s_ag", "xts0", "xts1", "fin0", "fin1",
                 "l2", "s_pub", "ys0", "ys1", "rs0", "rs1", "fA0", "fA1",
                 "g_w", "ga", "gb", "g_w2", "g_w2b", "g_l", "g_ow"):
        sems[name] = es.enter_context(nc.semaphore(name))

    psA = es.enter_context(nc.psum_tensor("psA", [128, 1024], F32))
    psB = es.enter_context(nc.psum_tensor("psB", [128, 1024], F32))
    psC = es.enter_context(nc.psum_tensor("psC", [128, 1024], F32))
    psD = es.enter_context(nc.psum_tensor("psD", [128, 1024], F32))

    C = Ctl()

    def wslc(dc, ft):
        return wq_sb[:, dc * 384 + ft * 128: dc * 384 + (ft + 1) * 128]

    att_sched = []
    for g in range(2):
        for j in range(NQT):
            nch = npk * (j + 1)
            for i in range(nch):
                att_sched.append((g, j, i, i >= nch - npk, i == 0, i == nch - 1))
    NCH = len(att_sched)

    def xt_load(eng, j):
        sem = "xf0" if j % 2 == 0 else "xf1"
        C.inc(eng.dma_start(out=xt_f32[j % 4][:, :],
                            in_=xTt[j * 128:(j + 1) * 128, :]), sem, 16)
        C.mark(f"xf_{j}", sem)

    def bc_load(eng, j):
        sem = "bc1" if j % 2 else "bc0"
        for (row, col, dst, nn) in stats_segments(j * DT, DT):
            C.inc(eng.dma_start(
                out=abc_sb[j % 4][:, dst:dst + nn].unsqueeze(1),
                in_=stats_out[row:row + 1, col:col + nn].partition_broadcast(128)),
                sem, 16)
            C.inc(eng.dma_start(
                out=cbc_sb[j % 4][:, dst:dst + nn].unsqueeze(1),
                in_=stats_out[row + 1:row + 2, col:col + nn].partition_broadcast(128)),
                sem, 16)
        C.mark(f"bcd_{j}", sem)

    def sync_stream(e):
        for ap_in, t in ((qkvb, qb_sb), (outb, ob_sb), (ffnb, fb_sb),
                         (identb, idb_sb), (identf, idf_sb),
                         (eidcol, eid_sb)):
            C.inc(e.dma_start(out=t[:, :], in_=ap_in[:, :]), "s_c", 16)
        C.mark("consts", "s_c")

        xt_load(e, 0)
        for i in range(0, NST, 2):
            if i >= 2:
                C.wait(e, f"sq_{i - 2}")
            C.inc(e.dma_start(out=trash_sb[:, (i % 2) * 1024:(i % 2) * 1024 + 1024],
                              in_=xshard[i * 128:(i + 1) * 128, :]), "s_x", 16)
            C.mark(f"xsh_{i}", "s_x")
        xt_load(e, 2)
        C.wait(e, "srow_ready")
        C.inc(e.dma_start(
            out=stats_in.ap().rearrange("r (g f) -> (r g) f", g=NST),
            in_=srow_sb[:, :]), "s_p", 16)
        C.mark("stats_pub", "s_p")
        C.wait(e, "cc_stats")

        nv65 = 65 * NVC
        nvS = 65 * (S // KC)
        for i in range(NDT):
            b = (i * DT) // S
            s0 = i * DT - b * S
            C.wait(e, f"rope_q_{i}")
            C.inc(e.dma_start(out=qkTd[b][:, s0:s0 + DT], in_=qst_q[i % 2][:, :]),
                  f"spq{i % 2}", 16)
            C.mark(f"spill_q_{i}", f"spq{i % 2}")
            C.wait(e, f"rope_k_{i}")
            C.inc(e.dma_start(out=qkTd[b][:, S + s0:S + s0 + DT], in_=qst_k[i % 2][:, :]),
                  f"spk{i % 2}", 16)
            C.mark(f"spill_k_{i}", f"spk{i % 2}")
            C.wait(e, f"vpst_{i}")
            c0 = (s0 // KC) * 65
            vsem = f"spv{i % 2}"
            C.inc(e.dma_start(out=vpd[b][:, c0:c0 + nv65],
                              in_=vp_st[i % 2][:, 0:nv65]), vsem, 16)
            C.inc(e.dma_start(out=vpd[b][:, nvS + c0: nvS + c0 + nv65],
                              in_=vp_st[i % 2][:, nv65:2 * nv65]), vsem, 16)
            C.mark(f"spill_{i}", vsem)
            if i + 4 < NDT and (i + 4) % 2 == 0:
                C.wait(e, f"xcast_{i}")
                C.wait(e, f"xcastv_{i}")
                xt_load(e, i + 4)
            if i == 0:
                bc_load(e, 1)
            if i + 3 < NDT and (i + 3) % 2 == 1:
                if i >= 2:
                    C.wait(e, f"fix_done_{i - 1}")
                bc_load(e, i + 3)

        for g in range(2):
            for li in (NDT - 1, NDT - 2):
                C.wait(e, f"spill_{li}")
                C.wait(e, f"spill_q_{li}")
                C.wait(e, f"spill_k_{li}")
            C.wait(e, f"qkvmm_{NDT - 1}")
            C.wait(e, f"vpst_{NDT - 1}")
            C.wait(e, f"xcast_{NDT - 1}")
            C.wait(e, f"xcastv_{NDT - 1}")
            if g == 0:
                C.inc(e.dma_start(out=mask_sb[:, :], in_=masks[:, :]), "s_c", 16)
                C.mark("maskld", "s_c")
            if g == 1:
                C.wait(e, f"PV_{NCH // 2 - 1}")
            gsem = f"s_g{g}"
            C.inc(e.dma_start(out=qkT_sb[g][:, :], in_=qkTd[g][:, :]), gsem, 16)
            C.mark(f"grp_{g}", gsem)
            for j in range(NQT):
                C.wait(e, f"ct_{g}_{j}")
                tok0 = g * S + j * QT
                csem = "s_ct" if j % 2 == 0 else "s_ct1"
                C.inc(e.dma_start(out=contrib[:, tok0:tok0 + QT],
                                  in_=ct2_sb[j % 2][:, 0:QT]), csem, 16)
                C.mark(f"ct_st_{g}_{j}", csem)
        C.mark("ct_all", "s_ct")
        C.mark("ct_all1", "s_ct1")

        C.wait(e, f"PV_{NCH - 1}")
        C.wait(e, "pcow")
        C.inc(e.dma_start(out=ow_sb[:, :], in_=owbf[:, :]), "g_ow", 16)
        C.mark("owld", "g_ow")

        rank = e.partition_id() if C.emit else 0
        baseA = e.compute_val(rank * pA) if C.emit else 0
        baseB = e.compute_val(rank * pB + MT) if C.emit else 0
        C.wait(e, "cc_ag")
        C.inc(e.dma_start(
            out=attnA_sb.rearrange("p (k t) -> p k t", k=8),
            in_=ag_out.ap().rearrange("(k p) t -> p k t", p=128)[:, :, ds(baseA, pA)]), "s_ag", 16)
        C.inc(e.dma_start(
            out=attnB_sb.rearrange("p (k t) -> p k t", k=8),
            in_=ag_out.ap().rearrange("(k p) t -> p k t", p=128)[:, :, ds(baseB, pB)]), "s_ag", 16)
        C.inc(e.dma_start(out=biasbc_sb[:, :].unsqueeze(1),
                          in_=ffnb_row[0:1, :].partition_broadcast(128)), "s_ag", 16)
        C.mark("biasbc", "s_ag")
        C.mark("ag_sl", "s_ag")
        for dc in range(8):
            for t in range(nta):
                k = dc * nta + t
                if k >= 2:
                    C.wait(e, f"x2A_{k - 2}")
                C.inc(e.dma_start(out=xts_sb[k % 2][:, 0:wa],
                                  in_=xTshard[dc * 128:(dc + 1) * 128,
                                              t * wa:(t + 1) * wa]), f"xts{k % 2}", 16)
                C.mark(f"xtsA_{k}", f"xts{k % 2}")
        off = pA
        for ti, wb in enumerate(wbs):
            for dc in range(8):
                k = ti * 8 + dc
                if k >= 2:
                    C.wait(e, f"x2B_{k - 2}")
                else:
                    C.wait(e, f"x2A_{8 * nta - 2 + k}")
                C.inc(e.dma_start(out=xts_sb[k % 2][:, 0:wb],
                                  in_=xTshard[dc * 128:(dc + 1) * 128, off:off + wb]),
                      f"xts{k % 2}", 16)
                C.mark(f"xtsB_{k}", f"xts{k % 2}")
            for sub in range(wb // 128):
                bt = (off - pA) // 128 + sub
                C.wait(e, f"finB_{bt}")
                C.inc(e.dma_start(out=out_shard[pA + bt * 128: pA + (bt + 1) * 128, :],
                                  in_=finB_sb[bt % 2][:, :]), f"fin{bt % 2}", 16)
                C.mark(f"outB_{bt}", f"fin{bt % 2}")
            off += wb
        C.mark("outB0", "fin0")
        C.mark("outB1", "fin1")

        C.wait(e, "l2cols")
        C.inc(e.dma_start(out=ln2st.ap().rearrange("g p -> p g"),
                          in_=scol_sb[:, 0:2 * NA],
                          allow_slow_non_contiguous=True), "l2", 16)
        C.mark("l2st_w", "l2")
        C.wait(e, "l2st_w")
        l2r = ln2st.ap().rearrange("(r g) f -> r (g f)", r=2)
        C.inc(e.dma_start(out=abc2_sb[:, :].unsqueeze(1),
                          in_=l2r[0:1, :].partition_broadcast(128)), "l2", 16)
        C.inc(e.dma_start(out=cbc2_sb[:, :].unsqueeze(1),
                          in_=l2r[1:2, :].partition_broadcast(128)), "l2", 16)
        C.mark("l2bc", "l2")

        C.wait(e, "ln2p")
        C.inc(e.dma_start(
            out=ln2_in.ap().rearrange("(c p) t -> p c t", p=128),
            in_=ln2p_sb.rearrange("p (c t) -> p c t", c=8)), "s_pub", 16)
        C.mark("ln2_pub", "s_pub")

        for g in range(NG):
            for hf in range(2):
                for tc in range(MGC):
                    kk = (g * 2 + hf) * MGC + tc
                    C.wait(e, f"ys_ev_{kk}")
                    C.inc(e.dma_start(
                        out=rsg_in[g][tc * 128:(tc + 1) * 128, hf * 512:(hf + 1) * 512],
                        in_=ys_st[kk % 2][:, :]), f"ys{kk % 2}", 16)
                    C.mark(f"ys_st_{kk}", f"ys{kk % 2}")
            C.mark(f"rsin_{g}_a", "ys0")
            C.mark(f"rsin_{g}_b", "ys1")
        if C.emit:
            for s in ("s_ct", "s_ct1", "ys0", "ys1", "s_p", "s_pub",
                      "spq0", "spq1", "spk0", "spk1", "spv0", "spv1", "l2",
                      "xf0", "bc1", "g_w2", "g_w2b", "g_ow", "s_ag"):
                if C.counts[s] > 0:
                    e.wait_ge(sems[s], C.counts[s])

    def gpsimd_stream(e):
        C.inc(e.dma_start(out=wq_sb.ap().rearrange("p (c f) -> p c f", c=8),
                          in_=qkvwT.rearrange("(c p) f -> p c f", p=128)), "g_w", 16)
        for i in range(NMT):
            C.inc(e.dma_start(out=iw_sb[:, 2 * i:2 * i + 2],
                              in_=midx[i * 128:(i + 1) * 128, :]), "g_w", 16)
            C.inc(e.dma_start(out=ewt_sb[:, 2 * i:2 * i + 2],
                              in_=mew[i * 128:(i + 1) * 128, :]), "g_w", 16)
        C.mark("wconsts", "g_w")
        C.mark("midx", "g_w")

        C.wait(e, "stats_pub")
        C.inc(e.collective_compute("AllGather", mybir.AluOpType.bypass,
                                   replica_groups=rg, ins=[stats_in.ap().opt()],
                                   outs=[stats_out.ap().opt()]), "cc", 1)
        C.mark("cc_stats", "cc")

        C.wait(e, f"xcast_{NDT - 1}")
        C.inc(e.dma_start(out=owbf.ap().rearrange("p (c d) -> p c d", c=8),
                          in_=outwT.rearrange("(c p) d -> p c d", p=128)), "pcw", 16)
        C.mark("pcow", "pcw")
        for ffc in range(FFC):
            C.inc(e.dma_start(
                out=w1bf.ap()[ffc * 128:(ffc + 1) * 128, :].rearrange(
                    "p (c f) -> p c f", c=8),
                in_=w1.rearrange("(c p) f -> p c f", p=128)[:, :, ffc * 128:(ffc + 1) * 128]),
                "pcw", 16)
            C.mark(f"pcw1_{ffc}", "pcw")
        for hf in range(2):
            C.inc(e.dma_start(
                out=w2bf.ap()[hf * 128:(hf + 1) * 128, :].rearrange(
                    "p (c d) -> p c d", c=FFC),
                in_=w2.rearrange("(c p) d -> p c d", p=128)[:, :, hf * 512:(hf + 1) * 512]),
                "pcw", 16)
            C.mark(f"pcw2_{hf}", "pcw")

        C.wait(e, "ct_all")
        C.wait(e, "ct_all1")
        C.inc(e.collective_compute("AllGather", mybir.AluOpType.bypass,
                                   replica_groups=rg, ins=[contrib.ap().opt()],
                                   outs=[ag_out.ap().opt()]), "cc", 1)
        C.mark("cc_ag", "cc")
        C.wait(e, "ln2_pub")
        C.inc(e.collective_compute("AllGather", mybir.AluOpType.bypass,
                                   replica_groups=rg, ins=[ln2_in.ap().opt()],
                                   outs=[ln2_out.ap().opt()]), "cc", 1)
        C.mark("cc_ln2", "cc")

        for g in range(NG):
            if g == 0:
                C.wait(e, "cc_ln2")
                C.wait(e, "outB0")
                C.wait(e, "outB1")
            else:
                C.wait(e, f"rsin_{g - 1}_a")
                C.wait(e, f"rsin_{g - 1}_b")
                C.inc(e.collective_compute("ReduceScatter", mybir.AluOpType.add,
                                           replica_groups=rg,
                                           ins=[rsg_in[g - 1].ap().opt()],
                                           outs=[rsg_out[g - 1].ap().opt()]), "cc", 1)
                C.mark(f"cc_rs_{g - 1}", "cc")
                C.wait(e, f"ysmm_g{g - 1}")
            for ffc in range(FFC):
                k = g * FFC + ffc
                if k >= 2:
                    C.wait(e, f"hmm_{k - 2}")
                if k < FFC:
                    C.wait(e, f"pcw1_{ffc}")
                gsem = "ga" if ffc % 2 == 0 else "gb"
                C.inc(e.dma_start(out=w1_bf[ffc % 2][:, :],
                                  in_=w1bf.ap()[ffc * 128:(ffc + 1) * 128, :]),
                      gsem, 16)
                C.mark(f"w1_{k}", gsem)
            for hf in range(2):
                hx = g * 2 + hf
                if hx >= 2:
                    C.wait(e, f"ysmm_h{hx - 2}")
                if g == 0:
                    C.wait(e, f"pcw2_{hf}")
                C.inc(e.dma_start(out=w2_bf[hx % 2][:, :],
                                  in_=w2bf.ap()[hf * 128:(hf + 1) * 128, :]),
                      "g_w2" if hx % 2 == 0 else "g_w2b", 16)
                C.mark(f"w2_{g}_{hf}", "g_w2" if hx % 2 == 0 else "g_w2b")

        g = NG - 1
        C.wait(e, f"rsin_{g}_a")
        C.wait(e, f"rsin_{g}_b")
        C.inc(e.collective_compute("ReduceScatter", mybir.AluOpType.add,
                                   replica_groups=rg, ins=[rsg_in[g].ap().opt()],
                                   outs=[rsg_out[g].ap().opt()]), "cc", 1)
        C.mark(f"cc_rs_{g}", "cc")
        if C.emit:
            for s in ("pcw", "ga", "gb", "g_w"):
                if C.counts[s] > 0:
                    e.wait_ge(sems[s], C.counts[s])

    def tensor_stream(e):
        C.wait(e, "wconsts")
        C.wait(e, "onesb_ready")
        mm = None
        for ft in range(3):
            for dc in range(8):
                mm = e.matmul(psD[0:128, ft:ft + 1], wslc(dc, ft), onesb_sb[0:128, 0:1],
                              start=(dc == 0), stop=(dc == 7))
        C.inc(mm, "te", 1)
        C.mark("colsum_mm", "te")

        C.wait(e, "scol_ready")
        C.wait(e, "consts")
        C.inc(e.transpose(psC[0:2 * NST, 0:128], scol_sb[:, 0:2 * NST], idf_sb[:, :]),
              "te", 1)
        C.mark("stats_tr", "te")

        for i in range(NDT):
            p = i % 2
            C.wait(e, f"xcast_{i}")
            C.wait(e, f"xcastv_{i}")
            if i == 0:
                C.wait(e, "srow_ready")
            for ft, (pst, lbl) in enumerate(((psA, "qev"), (psB, "kev"), (psC, "vev"))):
                if i >= 2:
                    C.wait(e, f"{lbl}_{i - 2}")
                tgt = pst[:, p * 512:p * 512 + 512]
                mm = None
                for dc in range(8):
                    mm = e.matmul(tgt, wslc(dc, ft), xt_bf[i % 2][:, dc * DT:(dc + 1) * DT],
                                  start=(dc == 0), stop=(dc == 7))
                C.inc(mm, "te", 1)
                C.mark(f"qkv_{i}_{ft}", "te")
            C.mark(f"qkvmm_{i}", "te")
            for ch in range(NVC):
                C.wait(e, f"vbf_{i}")
                gi = i * NVC + ch
                if gi >= 2:
                    pi, pch = (gi - 2) // NVC, (gi - 2) % NVC
                    C.wait(e, f"vtr_ev_{pi}_{pch}")
                sl = (gi % 2) * 1024 + 512
                C.inc(e.transpose(psD.ap().bitcast(BF16)[:, sl:sl + 128],
                                  v_bf[i % 2][:, ch * 128:(ch + 1) * 128], idb_sb[:, :]),
                      "te", 1)
                C.mark(f"vtr_{i}_{ch}", "te")

        pend = []
        nrm = [0]

        def emit_rcbc(gg, jj, pr, ci_last):
            qi = gg * NQT + jj
            pbank = psC if qi % 2 == 0 else psD
            x = nrm[0]
            C.wait(e, f"rc_{gg}_{jj}_{pr}")
            C.inc(e.matmul(pbank[0:64, pr * 512:pr * 512 + 512], onesf_sb[0:1, 0:64],
                           rcr_sb[0:1, (x % 2) * QT:(x % 2) * QT + QT],
                           start=True, stop=True), "te", 1)
            C.mark(f"rcbc_{gg}_{jj}_{pr}", "te")
            nrm[0] += 1

        def emit_PV(ci):
            g, j, i, masked, first, last = att_sched[ci]
            qi = g * NQT + j
            pso = psC if qi % 2 == 0 else psD
            if first:
                if qi == 1:
                    C.wait(e, f"vtr_ev_{NDT - 1}_{NVC - 1}")
                    C.wait(e, f"vpst_{NDT - 1}")
                if qi >= 2:
                    pg, pj = divmod(qi - 2, NQT)
                    C.wait(e, f"ouev_{pg}_{pj}")
                    C.wait(e, f"ctn_{2 * (qi - 2) + 1}")
            if ci == 0:
                C.wait(e, f"vtr_ev_{NDT - 1}_{NVC - 1}")
                C.wait(e, f"vpst_{NDT - 1}")
            C.wait(e, f"pm_{ci}" if masked else f"exp_{ci}")
            pb = pT_sb[ci % 4]
            e.matmul(pso[0:65, 0:QT], vpA_sb[g][:, i * 65:(i + 1) * 65], pb[:, 0:QT],
                     start=first, stop=last)
            C.inc(e.matmul(pso[0:65, QT:2 * QT], vpB_sb[g][:, i * 65:(i + 1) * 65],
                           pb[:, QT:2 * QT], start=first, stop=last), "te", 1)
            C.mark(f"PV_{ci}", "te")
            if last:
                pend.append((g, j, 0, ci))
                pend.append((g, j, 1, ci))

        pvq = []
        for ci, (g, j, i, masked, first, last) in enumerate(att_sched):
            if i == 0:
                if g == 1 and j == 0:
                    while pvq:
                        emit_PV(pvq.pop(0))
                    while pend:
                        emit_rcbc(*pend.pop(0))
                C.wait(e, f"grp_{g}")
                C.wait(e, f"grpv_{g}")
                if ci == 0:
                    C.wait(e, f"qkvmm_{NDT - 1}")
                    C.wait(e, f"fix_done_{NDT - 1}")
            if pend and i in (1, 2):
                emit_rcbc(*pend.pop(0))
            ps = psA if i % 2 == 0 else psB
            if ci >= 2:
                C.wait(e, f"exp_{ci - 2}")
            if ci == 0:
                C.wait(e, f"vtr_ev_{NDT - 1}_{NVC - 1}")
            qsl = slice(j * QT, (j + 1) * QT)
            e.matmul(ps[:, 0:QT], qkT_sb[g][0:64, S + i * KC:S + (i + 1) * KC],
                     qkT_sb[g][0:64, qsl], start=True, stop=True, tile_position=(0, 0))
            C.inc(e.matmul(ps[:, QT:2 * QT], qkT_sb[g][64:128, S + i * KC:S + (i + 1) * KC],
                           qkT_sb[g][64:128, qsl], start=True, stop=True,
                           tile_position=(64, 0)), "te", 1)
            C.mark(f"S_{ci}", "te")
            pvq.append(ci)
            if len(pvq) >= 3:
                emit_PV(pvq.pop(0))
                emit_PV(pvq.pop(0))
        while pvq:
            emit_PV(pvq.pop(0))
        while pend:
            emit_rcbc(*pend.pop(0))

        C.wait(e, "ag_sl")
        C.wait(e, "owld")
        C.wait(e, f"exp_{NCH - 1}")
        C.wait(e, f"ctn_{2 * NQT * 2 - 1}")
        for dc in range(8):
            for t in range(nta):
                k = dc * nta + t
                if k >= 2:
                    C.wait(e, f"x2A_{k - 2}")
                tgt = psA[:, (k % 2) * 512:(k % 2) * 512 + wa]
                mm = None
                for hc in range(8):
                    mm = e.matmul(tgt, ow_sb[:, hc * D + dc * 128: hc * D + (dc + 1) * 128],
                                  attnA_sb[:, hc * pA + t * wa: hc * pA + t * wa + wa],
                                  start=(hc == 0), stop=(hc == 7))
                C.inc(mm, "te", 1)
                C.mark(f"opA_{k}", "te")
        off = 0
        for ti, wb in enumerate(wbs):
            for dc in range(8):
                k = ti * 8 + dc
                if k >= 2:
                    C.wait(e, f"x2B_{k - 2}")
                else:
                    C.wait(e, f"x2A_{8 * nta - 2 + k}")
                tgt = psA[:, (k % 2) * 512:(k % 2) * 512 + wb]
                mm = None
                for hc in range(8):
                    mm = e.matmul(tgt, ow_sb[:, hc * D + dc * 128: hc * D + (dc + 1) * 128],
                                  attnB_sb[:, hc * pB + off: hc * pB + off + wb],
                                  start=(hc == 0), stop=(hc == 7))
                C.inc(mm, "te", 1)
                C.mark(f"opB_{k}", "te")
            C.wait(e, f"x2Bt_{ti}")
            for sub in range(wb // 128):
                bt = off // 128 + sub
                for dc in range(8):
                    gi = bt * 8 + dc
                    if gi >= 2:
                        C.wait(e, f"finB_ev_{(gi - 2) // 8}_{(gi - 2) % 8}")
                    C.inc(e.transpose(psB[:, (gi % 2) * 512:(gi % 2) * 512 + 128],
                                      x2Bt_sb[:, dc * 512 + sub * 128: dc * 512 + sub * 128 + 128],
                                      idf_sb[:, :]), "te", 1)
                    C.mark(f"trB_{bt}_{dc}", "te")
            off += wb

        C.wait(e, "x2A_all")
        C.wait(e, f"finB_{NBT - 1}")
        for at in range(NA):
            for dc in range(8):
                gi = at * 8 + dc
                if gi >= 2:
                    C.wait(e, f"trA_ev_{(gi - 2) // 8}_{(gi - 2) % 8}")
                C.inc(e.transpose(psB[:, (gi % 2) * 512:(gi % 2) * 512 + 128],
                                  x2A_sb[:, dc * pA + at * 128: dc * pA + at * 128 + 128],
                                  idf_sb[:, :]), "te", 1)
                C.mark(f"trA_{at}_{dc}", "te")

        for g in range(NG):
            C.wait(e, f"ln2g_{g}")
            if g == 0:
                C.wait(e, f"x2B_{len(wbs) * 8 - 1}")
            for ffc in range(FFC):
                k = g * FFC + ffc
                C.wait(e, f"w1_{k}")
                for hm in range(NMH):
                    if k * NMH + hm >= 2:
                        pk = k * NMH + hm - 2
                        C.wait(e, f"gelu_{pk // NMH}_{pk % NMH}")
                    sl = (ffc * NMH + hm) % 2
                    tgt = psA[:, sl * 512:sl * 512 + 512]
                    mm = None
                    for hc in range(8):
                        mm = e.matmul(tgt, w1_bf[ffc % 2][:, hc * 128:(hc + 1) * 128],
                                      ln2g_sb[:, hc * MG + hm * 512:hc * MG + (hm + 1) * 512],
                                      start=(hc == 0), stop=(hc == 7))
                    C.inc(mm, "te", 1)
                    C.mark(f"hps_{k}_{hm}", "te")
                C.mark(f"hmm_{k}", "te")
            for hf in range(2):
                hx = g * 2 + hf
                C.wait(e, f"w2_{g}_{hf}")
                C.wait(e, f"gelu_{g * FFC + FFC - 1}_{NMH - 1}")
                if g == 0 and hf == 0:
                    C.wait(e, f"trA_ev_{NA - 1}_7")
                for tc in range(MGC):
                    kk = hx * MGC + tc
                    if kk >= 2:
                        C.wait(e, f"ys_ev_{kk - 2}")
                    tgt = psB[:, (kk % 2) * 512:(kk % 2) * 512 + 512]
                    mm = None
                    for ffc in range(FFC):
                        mm = e.matmul(tgt, h_sb[:, ffc * MG + tc * 128:ffc * MG + (tc + 1) * 128],
                                      w2_bf[hx % 2][:, ffc * 512:(ffc + 1) * 512],
                                      start=(ffc == 0), stop=(ffc == FFC - 1))
                    C.inc(mm, "te", 1)
                    C.mark(f"ysmm_{kk}", "te")
                C.mark(f"ysmm_h{hx}", "te")
            C.mark(f"ysmm_g{g}", "te")

    def vector_stream(e):
        e.memset(onesf_sb[:, :], 1.0)
        C.mark("ones_ready", "ve")
        e.memset(eps_sb[:, :], EPS)
        C.mark("eps_ready", "ve")
        e.memset(onesb_sb[:, :], 1.0)
        e.memset(onesbr_sb[:, :], 1.0)
        C.mark("onesb_ready", "ve")
        for st in range(2):
            e.memset(vp_st[st].rearrange("p (c w) -> p c w", w=65)[:, :, 64:65], 1.0)
        C.mark("vpones", "ve")

        C.wait(e, "colsum_mm")
        e.tensor_scalar_mul(ncs_sb[:, 0:3], psD[0:128, 0:3], -1.0)
        C.mark("colsum_done", "ve")

        for i in range(NST):
            C.wait(e, f"xsh_{i}")
            e.reduce_sum(acc_sb[:, i:i + 1],
                         trash_sb[:, (i % 2) * 1024:(i % 2) * 1024 + 1024], axis=AX.X)
            C.mark(f"sum_{i}", "ve")
        C.wait(e, "sq_all")
        n = NST
        e.tensor_scalar_mul(acc_sb[:, 2 * n:3 * n], acc_sb[:, 0:n], 1.0 / D)
        e.tensor_scalar_mul(acc_sb[:, 3 * n:4 * n], acc_sb[:, n:2 * n], 1.0 / D)
        e.tensor_tensor(scol_sb[:, 0:n], acc_sb[:, 2 * n:3 * n],
                        acc_sb[:, 2 * n:3 * n], ALU.mult)
        e.tensor_tensor(acc_sb[:, 3 * n:4 * n], acc_sb[:, 3 * n:4 * n],
                        scol_sb[:, 0:n], ALU.subtract)
        C.mark("var_ready", "ve")
        C.wait(e, "rstd_done")
        e.tensor_tensor(scol_sb[:, n:2 * n], acc_sb[:, 2 * n:3 * n],
                        scol_sb[:, 0:n], ALU.mult)
        C.mark("scol_ready", "ve")
        C.wait(e, "stats_tr")
        e.tensor_copy(srow_sb[0:2 * NST, :], psC[0:2 * NST, 0:128])
        C.mark("srow_ready", "ve")

        C.wait(e, "midx")
        e.tensor_copy(idxf_sb[:, :], iw_sb[:, :])
        id3 = idxf_sb.ap().rearrange("p (n two) -> p n two", two=2)
        ew3 = ewt_sb.ap().rearrange("p (n two) -> p n two", two=2)
        e.scalar_tensor_tensor(mtmp_sb[:, :], id3[:, :, 0], eid_sb[:, 0:1],
                               ew3[:, :, 0], ALU.is_equal, ALU.mult)
        e.scalar_tensor_tensor(mcol_sb[:, :], id3[:, :, 1], eid_sb[:, 0:1],
                               ew3[:, :, 1], ALU.is_equal, ALU.mult)
        e.tensor_tensor(mcol_sb[:, :], mcol_sb[:, :], mtmp_sb[:, :], ALU.add)
        C.mark("mcol", "ve")

        for i in range(NDT):
            p = i % 2
            C.wait(e, f"xf_{i}")
            if i >= 2:
                C.wait(e, f"qkvmm_{i - 2}")
            e.tensor_copy(xt_bf[i % 2][:, 6 * DT:8 * DT], xt_f32[i % 4][:, 6 * DT:8 * DT])
            C.mark(f"xcastv_{i}", "ve")
            C.wait(e, f"bcd_{i}")
            vals = (valq_sb, valk_sb, valv_sb)
            for ft, (pst, lbl) in enumerate(((psA, "qev"), (psB, "kev"), (psC, "vev"))):
                C.wait(e, f"qkv_{i}_{ft}")
                if i >= 2:
                    C.wait(e, (f"shq_{i - 2}", f"shk_{i - 2}", f"vbf_{i - 2}")[ft])
                src_ps = pst[:, p * 512:p * 512 + 512]
                e.tensor_tensor(t1_sb[:, :], src_ps, abc_sb[i % 4][:, :], ALU.mult)
                C.mark(f"{lbl}_{i}", "ve")
                vb = vals[ft][i % 2]
                e.scalar_tensor_tensor(vb[:, :], cbc_sb[i % 4][:, :], ncs_sb[:, ft:ft + 1],
                                       t1_sb[:, :], ALU.mult, ALU.add)
                if ft < 2:
                    e.tensor_scalar_add(vb[:, :], vb[:, :], qb_sb[:, ft:ft + 1])
                C.mark(f"val{'qkv'[ft]}_{i}", "ve")
            C.mark(f"fix_done_{i}", "ve")
            for ft in range(2):
                C.wait(e, f"trig_{i}")
                C.wait(e, f"shq_{i}" if ft == 0 else f"shk_{i}")
                if i >= 2:
                    C.wait(e, f"spill_q_{i - 2}" if ft == 0 else f"spill_k_{i - 2}")
                tc = trig_sb[i % 4]
                vb = (valq_sb, valk_sb)[ft][i % 2]
                sh = (shq_sb, shk_sb)[ft][i % 2]
                e.tensor_tensor(t1_sb[:, :], vb[:, :], tc[:, 0:DT], ALU.mult)
                e.tensor_tensor(t3_sb[:, :], sh[:, :], tc[:, DT:2 * DT], ALU.mult)
                dst = qst_q[i % 2] if ft == 0 else qst_k[i % 2]
                e.tensor_tensor(dst[:, :], t1_sb[:, :], t3_sb[:, :], ALU.add)
                C.mark(f"rope_q_{i}" if ft == 0 else f"rope_k_{i}", "ve")

        nrmv = [0]
        for ci, (g, j, i, masked, first, last) in enumerate(att_sched):
            if masked:
                nch = npk * (j + 1)
                kk = i - (nch - npk)
                C.wait(e, "maskld")
                C.wait(e, f"exp_{ci}")
                e.tensor_tensor(pT_sb[ci % 4][:, :], pT_sb[ci % 4][:, :],
                                mask_sb[:, kk * 2 * QT:(kk + 1) * 2 * QT], ALU.mult)
                C.mark(f"pm_{ci}", "ve")
            if last:
                qi = g * NQT + j
                pso = psC if qi % 2 == 0 else psD
                C.wait(e, f"PV_{ci}")
                sA = (j % 2) * 2
                e.tensor_copy(outU_sb[0:65, sA * QT:(sA + 1) * QT], pso[0:65, 0:QT])
                e.tensor_copy(outU_sb[0:65, (sA + 1) * QT:(sA + 2) * QT],
                              pso[0:65, QT:2 * QT])
                C.mark(f"ouev_{g}_{j}", "ve")
                for pr2 in range(2):
                    x2 = nrmv[0] + pr2
                    e.reciprocal(rcr_sb[0:1, (x2 % 2) * QT:(x2 % 2) * QT + QT],
                                 outU_sb[64:65, (sA + pr2) * QT:(sA + pr2 + 1) * QT])
                    C.mark(f"rc_{g}_{j}_{pr2}", "ve")
                if (g, j) not in ((0, 0), (0, 1)):
                    pg, pj = (g, j - 2) if j >= 2 else (g - 1, NQT - 2 + j)
                    C.wait(e, f"ct_st_{pg}_{pj}")
                for pr in range(2):
                    x = nrmv[0]
                    C.wait(e, f"rcbc_{g}_{j}_{pr}")
                    pbank = psC if qi % 2 == 0 else psD
                    e.tensor_tensor(ct2_sb[j % 2][pr * 64:(pr + 1) * 64, 0:QT],
                                    outU_sb[0:64, (sA + pr) * QT:(sA + pr + 1) * QT],
                                    pbank[0:64, pr * 512:pr * 512 + 512], ALU.mult)
                    C.mark(f"ctn_{x}", "ve")
                    nrmv[0] += 1
                C.mark(f"ct_{g}_{j}", "ve")

        for dc in range(8):
            for t in range(nta):
                k = dc * nta + t
                C.wait(e, f"opA_{k}")
                C.wait(e, f"xtsA_{k}")
                src = psA[:, (k % 2) * 512:(k % 2) * 512 + wa]
                e.tensor_scalar_add(t1b_sb[:, 0:wa], src, ob_sb[:, dc:dc + 1])
                e.tensor_tensor(x2A_sb[:, dc * pA + t * wa: dc * pA + t * wa + wa],
                                t1b_sb[:, 0:wa], xts_sb[k % 2][:, 0:wa], ALU.add)
                C.mark(f"x2A_{k}", "ve")
        C.mark("x2A_all", "ve")
        off = 0
        for ti, wb in enumerate(wbs):
            for dc in range(8):
                k = ti * 8 + dc
                C.wait(e, f"opB_{k}")
                C.wait(e, f"xtsB_{k}")
                if ti >= 1 and dc == 0:
                    C.wait(e, f"trB_{off // 128 - 1}_7")
                src = psA[:, (k % 2) * 512:(k % 2) * 512 + wb]
                e.tensor_scalar_add(t1b_sb[:, 0:wb], src, ob_sb[:, dc:dc + 1])
                e.tensor_tensor(t1b_sb[:, 0:wb], t1b_sb[:, 0:wb], xts_sb[k % 2][:, 0:wb], ALU.add)
                e.tensor_scalar_add(x2Bt_sb[:, dc * 512:dc * 512 + wb],
                                    t1b_sb[:, 0:wb], fb_sb[:, dc:dc + 1])
                C.mark(f"x2B_{k}", "ve")
            C.mark(f"x2Bt_{ti}", "ve")
            for sub in range(wb // 128):
                bt = off // 128 + sub
                if bt >= 2:
                    C.wait(e, f"outB_{bt - 2}")
                for dc in range(8):
                    C.wait(e, f"trB_{bt}_{dc}")
                    gi = bt * 8 + dc
                    e.tensor_copy(finB_sb[bt % 2][:, dc * 128:(dc + 1) * 128],
                                  psB[:, (gi % 2) * 512:(gi % 2) * 512 + 128])
                    C.mark(f"finB_ev_{bt}_{dc}", "ve")
                C.mark(f"finB_{bt}", "ve")
            off += wb

        for at in range(NA):
            for dc in range(8):
                C.wait(e, f"trA_{at}_{dc}")
                gi = at * 8 + dc
                e.tensor_copy(x2At_sb[:, at * D + dc * 128: at * D + (dc + 1) * 128],
                              psB[:, (gi % 2) * 512:(gi % 2) * 512 + 128])
                C.mark(f"trA_ev_{at}_{dc}", "ve")

        for at in range(NA):
            e.reduce_sum(acc_sb[:, at:at + 1], x2At_sb[:, at * D:(at + 1) * D], axis=AX.X)
            C.mark(f"l2sum_{at}", "ve")
        C.wait(e, "l2sq_all")
        n = NA
        e.tensor_scalar_mul(acc_sb[:, 2 * n:3 * n], acc_sb[:, 0:n], 1.0 / D)
        e.tensor_scalar_mul(acc_sb[:, 3 * n:4 * n], acc_sb[:, n:2 * n], 1.0 / D)
        e.tensor_tensor(scol_sb[:, 0:n], acc_sb[:, 2 * n:3 * n],
                        acc_sb[:, 2 * n:3 * n], ALU.mult)
        e.tensor_tensor(acc_sb[:, 3 * n:4 * n], acc_sb[:, 3 * n:4 * n],
                        scol_sb[:, 0:n], ALU.subtract)
        C.mark("l2var", "ve")
        C.wait(e, "l2rstd")
        e.tensor_copy(scol_sb[:, 0:n], acc_sb[:, 0:n])
        e.tensor_tensor(scol_sb[:, n:2 * n], acc_sb[:, 2 * n:3 * n],
                        acc_sb[:, 0:n], ALU.mult)
        C.mark("l2cols", "ve")

        C.wait(e, "l2bc")
        for dc in range(8):
            e.tensor_tensor(t1b_sb[:, 0:pA] if pA <= 512 else None,
                            x2A_sb[:, dc * pA:(dc + 1) * pA], abc2_sb[:, 0:pA], ALU.mult)
            e.tensor_tensor(ln2p_sb[:, dc * pA:(dc + 1) * pA], t1b_sb[:, 0:pA],
                            cbc2_sb[:, 0:pA], ALU.subtract)
        C.mark("ln2p", "ve")

        C.wait(e, "mcol")
        for g in range(NG):
            for hf in range(2):
                for tc in range(MGC):
                    kk = (g * 2 + hf) * MGC + tc
                    C.wait(e, f"ysmm_{kk}")
                    if kk >= 2:
                        C.wait(e, f"ys_st_{kk - 2}")
                    src = psB[:, (kk % 2) * 512:(kk % 2) * 512 + 512]
                    mi = NA * tc + g
                    e.tensor_scalar_mul(ys_st[kk % 2][:, :], src,
                                        mcol_sb[:, mi:mi + 1])
                    C.mark(f"ys_ev_{kk}", "ve")

        C.wait(e, "outB0")
        C.wait(e, "outB1")
        for g in range(NG):
            C.wait(e, f"rs_ld_{g}")
            C.wait(e, "biasbc")
            if g >= 2:
                C.wait(e, f"outA_{g - 2}")
            e.tensor_tensor(finA_sb[g % 2][:, :], rs_sb[g % 2][:, :],
                            x2At_sb[:, g * D:(g + 1) * D], ALU.add)
            e.tensor_tensor(finA_sb[g % 2][:, :], finA_sb[g % 2][:, :],
                            biasbc_sb[:, :], ALU.add)
            C.mark(f"finA_{g}", "ve")

    def scalar_stream(e):
        C.inc(e.dma_start(out=trash_sb[:, 1024:2048],
                          in_=xshard[128:256, :]), "x_s2", 16)
        C.mark("xsh_1", "x_s2")
        xt_load(e, 1)
        xt_load(e, 3)
        for i in range(NST):
            C.wait(e, f"sum_{i}")
            sl = trash_sb[:, (i % 2) * 1024:(i % 2) * 1024 + 1024]
            e.activation(sl, sl, AF.Square,
                         accum_out=acc_sb[:, NST + i:NST + i + 1])
            C.mark(f"sq_{i}", "sc")
            if i % 2 == 1 and i + 2 < NST:
                C.wait(e, f"sq_{i}")
                C.inc(e.dma_start(
                    out=trash_sb[:, 1024:2048],
                    in_=xshard[(i + 2) * 128:(i + 3) * 128, :]), "x_s2", 16)
                C.mark(f"xsh_{i + 2}", "x_s2")
        C.mark("sq_all", "sc")
        C.wait(e, "var_ready")
        C.wait(e, "eps_ready")
        n = NST
        e.activation(trash_sb[:, 0:n], acc_sb[:, 3 * n:4 * n], AF.Ln, bias=eps_sb[:, 0:1])
        e.activation(scol_sb[:, 0:n], trash_sb[:, 0:n], AF.Exp, scale=-0.5)
        C.mark("rstd_done", "sc")

        for i in range(NDT):
            b = (i * DT) // S
            s0 = i * DT - b * S
            C.wait(e, f"xf_{i}")
            if i >= 2:
                C.wait(e, f"qkvmm_{i - 2}")
            e.copy(xt_bf[i % 2][:, 0:6 * DT], xt_f32[i % 4][:, 0:6 * DT])
            C.mark(f"xcast_{i}", "sc")
            if i == 0:
                for jt in (0, 1):
                    jb = (jt * DT) % S // DT
                    C.inc(e.dma_start(out=trig_sb[jt][:, :],
                                      in_=trigT[jb * 128:(jb + 1) * 128, :]),
                          f"trig{jt % 2}", 16)
                    C.mark(f"trig_{jt}", f"trig{jt % 2}")
            if i + 2 < NDT:
                if i >= 2:
                    C.wait(e, f"rope_q_{i - 2}")
                    C.wait(e, f"rope_k_{i - 2}")
                jt = i + 2
                s2 = jt * DT - ((jt * DT) // S) * S
                jb = s2 // DT
                C.inc(e.dma_start(out=trig_sb[jt % 4][:, :],
                                  in_=trigT[jb * 128:(jb + 1) * 128, :]),
                      f"trig{jt % 2}", 16)
                C.mark(f"trig_{jt}", f"trig{jt % 2}")
            if i % 2 == 0:
                if i == 0:
                    C.wait(e, "cc_stats")
                    bc_load(e, 0)
                    bc_load(e, 2)
                elif i + 2 < NDT:
                    C.wait(e, f"fix_done_{i - 2}")
                    bc_load(e, i + 2)
            if i % 2 == 1 and i + 4 < NDT:
                C.wait(e, f"xcast_{i}")
                C.wait(e, f"xcastv_{i}")
                xt_load(e, i + 4)
            for ft in range(2):
                C.wait(e, f"val{'qk'[ft]}_{i}")
                if i >= 2:
                    C.wait(e, f"rope_q_{i - 2}" if ft == 0 else f"rope_k_{i - 2}")
                sh = (shq_sb, shk_sb)[ft][i % 2]
                vb = (valq_sb, valk_sb)[ft][i % 2]
                e.copy(sh[0:32, :], vb[32:64, :])
                e.copy(sh[32:64, :], vb[0:32, :])
                e.copy(sh[64:96, :], vb[96:128, :])
                e.copy(sh[96:128, :], vb[64:96, :])
                C.mark(f"shq_{i}" if ft == 0 else f"shk_{i}", "sc")
            C.wait(e, f"valv_{i}")
            if i >= 2:
                C.wait(e, f"vtr_{i - 2}_{NVC - 1}")
            e.activation(v_bf[i % 2][:, :], valv_sb[i % 2][:, :], AF.Identity,
                         bias=qb_sb[:, 2:3])
            C.mark(f"vbf_{i}", "sc")
            if i >= 2:
                C.wait(e, f"spill_{i - 2}")
            pd = psD.ap().bitcast(BF16)
            for ch in range(NVC):
                C.wait(e, f"vtr_{i}_{ch}")
                gi = i * NVC + ch
                sl = (gi % 2) * 1024 + 512
                e.copy(vp_st[i % 2][:, ch * 65:ch * 65 + 64], pd[:, sl:sl + 64])
                e.copy(vp_st[i % 2][:, 65 * NVC + ch * 65: 65 * NVC + ch * 65 + 64],
                       pd[:, sl + 64:sl + 128])
                C.mark(f"vtr_ev_{i}_{ch}", "sc")
            C.mark(f"vpst_{i}", "sc")

        def vp_reload(g):
            nvS = 65 * (S // KC)
            for li in (NDT - 1, NDT - 2):
                C.wait(e, f"spill_{li}")
                C.wait(e, f"spill_q_{li}")
                C.wait(e, f"spill_k_{li}")
            C.wait(e, f"qkvmm_{NDT - 1}")
            C.wait(e, f"vpst_{NDT - 1}")
            C.wait(e, f"xcast_{NDT - 1}")
            C.wait(e, f"xcastv_{NDT - 1}")
            if g == 1:
                C.wait(e, f"PV_{NCH // 2 - 1}")
            C.inc(e.dma_start(out=vpA_sb[g][:, :], in_=vpd[g][:, 0:nvS]), "s_gv", 16)
            C.inc(e.dma_start(out=vpB_sb[g][:, :], in_=vpd[g][:, nvS:2 * nvS]), "s_gv", 16)
            C.mark(f"grpv_{g}", "s_gv")

        vp_reload(0)

        for lbl in (f"spill_q_{NDT - 1}", f"spill_q_{NDT - 2}",
                    f"spill_k_{NDT - 1}", f"spill_k_{NDT - 2}",
                    f"spill_{NDT - 1}", f"spill_{NDT - 2}", f"xcastv_{NDT - 1}"):
            C.wait(e, lbl)
        for ci, (g, j, i, masked, first, last) in enumerate(att_sched):
            if ci == NCH // 2:
                vp_reload(1)
            C.wait(e, f"S_{ci}")
            if ci >= 4:
                C.wait(e, f"PV_{ci - 4}")
            ps = psA if i % 2 == 0 else psB
            e.activation(pT_sb[ci % 4][:, :], ps[:, :], AF.Exp, scale=0.125)
            C.mark(f"exp_{ci}", "sc")

        for at in range(NA):
            C.wait(e, f"l2sum_{at}")
            e.activation(trash_sb[:, 0:D], x2At_sb[:, at * D:(at + 1) * D],
                         AF.Square, accum_out=acc_sb[:, NA + at:NA + at + 1])
            C.mark(f"l2sq_{at}", "sc")
        C.mark("l2sq_all", "sc")
        C.wait(e, "l2var")
        n = NA
        e.activation(trash_sb[:, 0:n], acc_sb[:, 3 * n:4 * n], AF.Ln, bias=eps_sb[:, 0:1])
        e.activation(acc_sb[:, 0:n], trash_sb[:, 0:n], AF.Exp, scale=-0.5)
        C.mark("l2rstd", "sc")

        for g in range(NG):
            if g == 0:
                C.wait(e, "cc_ln2")
                C.wait(e, "outB0")
                C.wait(e, "outB1")
            else:
                C.wait(e, f"hmm_{g * FFC - 1}")
            for r in range(NC):
                C.inc(e.dma_start(
                    out=ln2g_sb.rearrange("p (c t) -> p c t", c=8)[:, :, r * 128:(r + 1) * 128],
                    in_=ln2_out.ap().rearrange("(r c p) t -> p (r c) t", p=128, c=8)[
                        :, r * 8:(r + 1) * 8, 128 * g:128 * (g + 1)]), "g_l", 16)
            C.mark(f"ln2g_{g}", "g_l")
            if g >= 1:
                C.wait(e, f"ysmm_g{g - 1}")
            for ffc in range(FFC):
                k = g * FFC + ffc
                for hm in range(NMH):
                    C.wait(e, f"hps_{k}_{hm}")
                    sl = (ffc * NMH + hm) % 2
                    e.activation(h_sb[:, ffc * MG + hm * 512:ffc * MG + (hm + 1) * 512],
                                 psA[:, sl * 512:sl * 512 + 512], AF.Gelu)
                    C.mark(f"gelu_{k}_{hm}", "sc")
            if g >= 1:
                gg = g - 1
                C.wait(e, f"cc_rs_{gg}")
                if gg >= 2:
                    C.wait(e, f"finA_{gg - 2}")
                C.inc(e.dma_start(out=rs_sb[gg % 2][:, :], in_=rsg_out[gg][:, :]),
                      f"rs{gg % 2}", 16)
                C.mark(f"rs_ld_{gg}", f"rs{gg % 2}")
        gg = NG - 1
        C.wait(e, f"cc_rs_{gg}")
        if gg >= 2:
            C.wait(e, f"finA_{gg - 2}")
        C.inc(e.dma_start(out=rs_sb[gg % 2][:, :], in_=rsg_out[gg][:, :]),
              f"rs{gg % 2}", 16)
        C.mark(f"rs_ld_{gg}", f"rs{gg % 2}")
        for g in range(NG):
            C.wait(e, f"finA_{g}")
            C.inc(e.dma_start(out=out_shard[g * 128:(g + 1) * 128, :],
                              in_=finA_sb[g % 2][:, :]), f"fA{g % 2}", 16)
            C.mark(f"outA_{g}", f"fA{g % 2}")
        if C.emit:
            for s in ("fA0", "fA1", "rs0", "rs1", "xf1", "bc0",
                      "trig0", "trig1", "x_s2", "s_gv", "g_l"):
                if C.counts[s] > 0:
                    e.wait_ge(C.sems[s], C.counts[s])

    streams = {
        "sy": sync_stream, "gp": gpsimd_stream, "te": tensor_stream,
        "ve": vector_stream, "sc": scalar_stream,
    }

    for emit in (False, True):
        C.start_pass(emit, sems)
        if not emit:
            sync_stream(_DummyEng())
            gpsimd_stream(_DummyEng())
            tensor_stream(_DummyEng())
            vector_stream(AutoEng(_DummyEng(), C, "ve"))
            scalar_stream(AutoEng(_DummyEng(), C, "sc"))
        else:
            block = es.enter_context(nc.Block())
            block.sync(lambda e: sync_stream(e))
            block.gpsimd(lambda e: gpsimd_stream(e))
            block.tensor(lambda e: tensor_stream(e))
            block.vector(lambda e: vector_stream(AutoEng(e, C, "ve")))
            block.scalar(lambda e: scalar_stream(AutoEng(e, C, "sc")))
    es.close()
    return nc


def make_inputs(cfg: Cfg, x, qkv_w, qkv_b, out_w, out_b, ew, idx, w1, w2, fbias):
    S, T, MT = cfg.S, cfg.T, cfg.MT
    pA, pB = cfg.pA, cfg.pB
    f32 = np.float32
    bf16 = ml_dtypes.bfloat16
    xf = np.ascontiguousarray(np.asarray(x, dtype=f32).reshape(T, D))
    NDT = T // 512
    xTt = np.ascontiguousarray(
        xf.T.reshape(8, 128, NDT, 512).transpose(2, 1, 0, 3).reshape(NDT * 128, 4096))
    outwT = np.ascontiguousarray(np.asarray(out_w, dtype=f32).T)
    outb_c = np.ascontiguousarray(np.asarray(out_b, dtype=f32).reshape(8, 128).T)
    ffnb_c = np.ascontiguousarray(np.asarray(fbias, dtype=f32).reshape(8, 128).T)
    ffnb_r = np.ascontiguousarray(np.asarray(fbias, dtype=f32).reshape(1, D))
    inv = 1.0 / (10000.0 ** (np.arange(32, dtype=f32) / np.float32(32.0)))
    ang = np.arange(S, dtype=f32)[:, None] * inv[None, :]
    cosS = np.cos(ang).T.astype(f32)
    sinS = np.sin(ang).T.astype(f32)
    cosTm = np.tile(cosS, (4, 1))
    sinTm = np.concatenate([-sinS, sinS, -sinS, sinS], axis=0)
    NJT = S // 512
    trigT = np.ascontiguousarray(np.concatenate(
        [cosTm.reshape(128, NJT, 512).transpose(1, 0, 2),
         sinTm.reshape(128, NJT, 512).transpose(1, 0, 2)], axis=2).reshape(NJT * 128, 1024))
    mk = []
    for kk in range(4):
        m = (np.arange(128)[:, None] + 128 * kk <= np.arange(cfg.QT)[None, :])
        mk.append(np.concatenate([m, m], axis=1))
    masks = np.ascontiguousarray(np.concatenate(mk, axis=1).astype(bf16))
    identb = np.eye(128, dtype=bf16)
    identf = np.eye(128, dtype=f32)
    qkv_w = np.asarray(qkv_w, dtype=f32)
    qkv_b = np.asarray(qkv_b, dtype=f32)
    w1 = np.asarray(w1, dtype=f32)
    w2 = np.asarray(w2, dtype=f32)

    in_maps = []
    for c in range(NC):
        Rc = np.r_[pA * c:pA * (c + 1), MT + pB * c: MT + pB * (c + 1)]
        sel = np.r_[128 * c:128 * (c + 1)]
        rows = np.r_[sel, 1024 + sel, 2048 + sel]
        in_maps.append({
            "xTt": xTt,
            "xshard": np.ascontiguousarray(xf[Rc]),
            "xTshard": np.ascontiguousarray(xf[Rc].T),
            "qkvwT": np.ascontiguousarray(qkv_w[rows].T),
            "qkvb": np.ascontiguousarray(qkv_b[rows].reshape(3, 128).T),
            "outwT": outwT,
            "outb": outb_c,
            "ffnb": ffnb_c,
            "ffnb_row": ffnb_r,
            "w1": np.ascontiguousarray(w1[c]),
            "w2": np.ascontiguousarray(w2[c]),
            "midx": np.ascontiguousarray(np.asarray(idx, dtype=np.int32)),
            "mew": np.ascontiguousarray(np.asarray(ew, dtype=f32)),
            "eidcol": np.full((128, 1), float(c), dtype=f32),
            "trigT": trigT,
            "masks": masks,
            "identb": identb,
            "identf": identf,
        })
    return in_maps


def assemble(cfg: Cfg, results):
    out = np.empty((cfg.T, D), dtype=np.float32)
    for c in range(NC):
        Rc = np.r_[cfg.pA * c:cfg.pA * (c + 1),
                   cfg.MT + cfg.pB * c: cfg.MT + cfg.pB * (c + 1)]
        out[Rc] = results[c]["out_shard"]
    return out.reshape(2, cfg.S, D)


def kernel(L0_attn_x, L0_attn_qkv_w, L0_attn_qkv_b, L0_attn_out_w, L0_attn_out_b,
           L0_ffn_scores, L0_ffn_expert_weights, L0_ffn_expert_indices,
           L0_ffn_w1, L0_ffn_w2, L0_ffn_bias):
    cfg = Cfg(S=np.asarray(L0_attn_x).shape[1], FF=np.asarray(L0_ffn_w1).shape[2])
    nc = build_program(cfg)
    in_maps = make_inputs(cfg, L0_attn_x, L0_attn_qkv_w, L0_attn_qkv_b,
                          L0_attn_out_w, L0_attn_out_b, L0_ffn_expert_weights,
                          L0_ffn_expert_indices, L0_ffn_w1, L0_ffn_w2, L0_ffn_bias)
    res = run_bass_kernel_spmd(nc, in_maps, core_ids=list(range(NC)))
    return assemble(cfg, res.results)
